# version 10
# baseline (speedup 1.0000x reference)
import numpy as np
import ml_dtypes

import concourse.bass as bass
import concourse.mybir as mybir
from concourse.aot_env import allow_ici_all_to_all
from concourse.bass_utils import run_bass_kernel_spmd

F32 = mybir.dt.float32
BF16 = mybir.dt.bfloat16
AF = mybir.ActivationFunctionType
ALU = mybir.AluOpType

D = 1024
S = 8192
TOK = 2048
NCH = 8
DH = 4096
NHC = 32
TT = 512
NTT = TOK // TT
QT = 1024
NQT = S // QT
EPS = 1e-5

ALL_PHASES = ("attnA", "ffn1", "attnB", "ffn2")
KEEP_PHASES = ALL_PHASES

BFD = ml_dtypes.bfloat16


class Ch:
    def __init__(self, nc, name, kind):
        self.h = nc.alloc_semaphore(name)
        self.n = 0
        self.kind = kind

    def mark(self, k=1):
        self.n += k
        return (self, self.n)

    def now(self):
        return (self, self.n)


class MK:
    ENGS = ("sp", "te", "act", "dve", "gp")

    def __init__(self):
        self.nc = bass.Bass("TRN2", target_bir_lowering=False, debug=False)
        self.q = {e: [] for e in self.ENGS}
        self.waited = {e: {} for e in self.ENGS}
        self.pending = {e: {} for e in self.ENGS}

    def emit(self, eng, fn, waits=(), inc=None, k=1):
        merged = dict(self.pending[eng])
        self.pending[eng] = {}
        for m in waits:
            if m is None:
                continue
            ch, v = m
            if v > merged.get(ch, 0):
                merged[ch] = v
        wl = []
        for ch, v in merged.items():
            if ch.kind == eng:
                continue
            if self.waited[eng].get(ch, -1) >= v:
                continue
            self.waited[eng][ch] = v
            wl.append((ch.h, v))
        mark = inc.mark(k) if inc is not None else None
        inch = inc.h if inc is not None else None

        def thunk(e, wl=wl, fn=fn, inch=inch, k=k):
            for h, v in wl:
                e.wait_ge(h, v)
            inst = fn(e)
            if inch is not None:
                inst.then_inc(inch, k)

        self.q[eng].append(thunk)
        return mark

    def raw_wait(self, eng, waits):
        wl = []
        for m in waits:
            if m is None:
                continue
            ch, v = m
            wl.append((ch.h, v))

        def thunk(e, wl=wl):
            for h, v in wl:
                e.wait_ge(h, v)

        self.q[eng].append(thunk)

    def join(self, engines=("sp", "te", "act", "dve", "gp")):
        chans = [self.cPE, self.cACT, self.cDVE, self.dX, self.dS]
        tgt = {ch: ch.n for ch in chans if ch.n > 0}
        for e in engines:
            for ch, v in tgt.items():
                if v > self.pending[e].get(ch, 0):
                    self.pending[e][ch] = v

    def dma(self, eng, out_ap, in_ap, ch, waits=()):
        return self.emit(eng, lambda e: e.dma_start(out=out_ap, in_=in_ap),
                         waits=waits, inc=ch, k=16)

    def mm_group(self, pairs, dst, waits_first=(), tile_position=None):
        n = len(pairs)
        mark = None
        for i, (lhsT, rhs) in enumerate(pairs):
            mark = self.emit(
                "te",
                (lambda e, lhsT=lhsT, rhs=rhs, i=i, n=n, dst=dst, tp=tile_position:
                 e.matmul(dst, lhsT, rhs, start=(i == 0), stop=(i == n - 1),
                          tile_position=tp)),
                waits=waits_first if i == 0 else (),
                inc=self.cPE if i == n - 1 else None)
        return mark

    def build(self):
        nc = self.nc
        self.cPE = Ch(nc, "cPE", "te")
        self.cACT = Ch(nc, "cACT", "act")
        self.cDVE = Ch(nc, "cDVE", "dve")
        self.dX = Ch(nc, "dX", "dma")
        self.dS = Ch(nc, "dS", "dma")
        self.dW = Ch(nc, "dW", "dma")
        self.cAG = [Ch(nc, f"cAG{i}", "dma") for i in range(4)]

        self.declare_io()
        self.alloc_sbuf()
        self.PS = [nc.alloc_psum_tensor(f"PS{i}", [128, 1024], F32) for i in range(4)]
        self.sfree = {}
        self.p_prologue()
        if "attnA" in KEEP_PHASES:
            self.p_attn(True)
        if "ffn1" in KEEP_PHASES:
            self.p_ffn()
        if "attnB" in KEEP_PHASES:
            self.p_attn(False)
        if "ffn2" in KEEP_PHASES:
            self.p_ffn()
        self.p_epilogue()

        with nc.Block() as block:
            @block.sync
            def _(e):
                for f in self.q["sp"]:
                    f(e)

            @block.tensor
            def _(e):
                for f in self.q["te"]:
                    f(e)

            @block.scalar
            def _(e):
                for f in self.q["act"]:
                    f(e)

            @block.vector
            def _(e):
                for f in self.q["dve"]:
                    f(e)

            @block.gpsimd
            def _(e):
                for f in self.q["gp"]:
                    f(e)
        return nc

    def declare_io(self):
        nc = self.nc
        dt = nc.dram_tensor
        self.x_in = dt("x0f", [128, NCH * TOK], F32, kind="ExternalInput")
        self.y_out = dt("y0f", [128, NCH * TOK], F32, kind="ExternalOutput")
        self.qwa = dt("qwa", [128, 2 * D], F32, kind="ExternalInput")
        self.kwa = dt("kwa", [128, 2 * D], F32, kind="ExternalInput")
        self.vwa = dt("vwa", [128, 2 * D], F32, kind="ExternalInput")
        self.owa = dt("owa", [128, 8 * D], F32, kind="ExternalInput")
        self.qwb = dt("qwb", [128, 2 * D], F32, kind="ExternalInput")
        self.kwb = dt("kwb", [128, 2 * D], F32, kind="ExternalInput")
        self.vwb = dt("vwb", [128, 2 * D], F32, kind="ExternalInput")
        self.owb = dt("owb", [128, 8 * D], F32, kind="ExternalInput")
        self.bqc = dt("bqc", [128, 2], F32, kind="ExternalInput")
        self.bkc = dt("bkc", [128, 2], F32, kind="ExternalInput")
        self.bvc = dt("bvc", [128, 256], F32, kind="ExternalInput")
        self.boc = dt("boc", [128, 8], F32, kind="ExternalInput")
        self.w1s = dt("w1s", [NHC * 128, D], F32, kind="ExternalInput")
        self.w2s = dt("w2s", [NHC * 128, D], F32, kind="ExternalInput")
        self.h1s = dt("h1s", [NHC * 128, DH], F32, kind="ExternalInput")
        self.h2s = dt("h2s", [NHC * 128, DH], F32, kind="ExternalInput")
        self.w3s = dt("w3s", [NCH * 128, DH], F32, kind="ExternalInput")
        self.s1c = dt("s1c", [128, NHC], F32, kind="ExternalInput")
        self.s2c = dt("s2c", [128, NHC], F32, kind="ExternalInput")
        self.sh1c = dt("sh1c", [128, NHC], F32, kind="ExternalInput")
        self.sh2c = dt("sh2c", [128, NHC], F32, kind="ExternalInput")
        self.s3c = dt("s3c", [128, NCH], F32, kind="ExternalInput")
        self.maskA_in = dt("maskA", [128, 1920], BF16, kind="ExternalInput")
        self.maskW_in = dt("maskW", [128, 1024], BF16, kind="ExternalInput")
        self.maskA8_in = dt("maskA8", [128, 1024], BF16, kind="ExternalInput")
        self.alibi_in = dt("alibi", [128, 3], F32, kind="ExternalInput")
        self.w1c = dt("w1c", [NHC * 128, D], BF16, kind="Internal")
        self.w2c = dt("w2c", [NHC * 128, D], BF16, kind="Internal")
        self.h1c = dt("h1c", [NHC * 128, DH], BF16, kind="Internal")
        self.h2c = dt("h2c", [NHC * 128, DH], BF16, kind="Internal")
        self.w3c = dt("w3c", [NCH * 128, DH], BF16, kind="Internal")
        self.ag_in = dt("ag_in", [128, NCH * TOK], BF16, kind="Internal")
        self.ag_out = dt("ag_out", [512, NCH * TOK], BF16, kind="Internal")
        self.m_in = dt("m_in", [128, 2 * S], BF16, kind="Internal")
        self.m_out = dt("m_out", [512, 2 * S], BF16, kind="Internal")

    def alloc_sbuf(self):
        nc = self.nc
        off = [0]
        SZ = 212448

        def place(name, shape, dtype, at=None):
            per = int(np.prod(shape[1:])) * mybir.dt.size(dtype)
            per = (per + 31) // 32 * 32
            if at is None:
                at = off[0]
                off[0] = at + per
            else:
                off[0] = max(off[0], at + per)
            assert at + per <= SZ, f"{name} overflows arena: {at + per}"
            return nc.alloc_sbuf_tensor_at(name, shape, dtype, offset=self.base + at)

        arena = nc.alloc_sbuf_tensor("arena", [128, SZ], mybir.dt.uint8)
        self.base = nc.lookup_mloc(arena).addr

        self.X = place("X", [128, NCH * TOK], F32)
        r1 = off[0]
        self.K_sb = place("K_sb", [128, 2 * S], BF16)
        self.h0 = place("h0", [128, NHC * TT], BF16, at=r1)
        self.ow_sb = place("ow_sb", [128, 8 * D], BF16, at=r1)
        off[0] = r1 + 33280
        r2 = off[0]
        self.V_sb = place("V_sb", [128, 64 * 260], BF16)
        self.g1 = place("g1", [128, NHC * TT], BF16, at=r2)
        off[0] = r2 + 33280
        r3 = off[0]
        self.xqq = place("xqq", [128, NCH * QT], BF16)
        self.xq = [place("xq0", [128, NCH * TT], BF16, at=r3),
                   place("xq1", [128, NCH * TT], BF16, at=r3 + 8192)]
        self.mg = [place("mg0", [128, NCH * TT], BF16, at=r3),
                   place("mg1", [128, NCH * TT], BF16, at=r3 + 8192)]
        self.xn_t = [place("xn0", [128, NCH * TT], BF16, at=r3),
                     place("xn1", [128, NCH * TT], BF16, at=r3 + 8192)]
        self.wstrip = [place("ws0", [128, NHC * 128], BF16, at=r3 + 16384),
                       place("ws1", [128, NHC * 128], BF16, at=r3 + 24576)]
        self.Q_sb = place("Q_sb", [128, 2 * QT], BF16, at=r3 + 16384)
        self.P_sb = [[place(f"P{p}{b}", [128, QT], BF16, at=r3 + 20480 + (2 * p + b) * 2048)
                      for b in range(2)] for p in range(2)]
        self.merged = place("merged", [128, 2 * QT], BF16, at=r3 + 28672)
        off[0] = r3 + 32768
        self.qw_sb = place("qw_sb", [128, 2 * D], BF16)
        self.kw_sb = place("kw_sb", [128, 2 * D], BF16)
        self.vw_sb = place("vw_sb", [128, 8 * 256], BF16)
        self.maskA = place("maskA", [128, 1920], BF16)
        self.maskW = place("maskW", [128, 1024], BF16)
        self.maskA8 = place("maskA8", [128, 1024], BF16)
        self.zeroP = place("zeroP", [128, 512], BF16)
        self.alibi = place("alibi", [128, 3], F32)
        self.ones_bf = place("ones_bf", [1, 128], BF16)
        self.onesK = place("onesK", [128, 1], F32)
        self.onesKb = place("onesKb", [128, 1], BF16)
        self.ones_fr = place("ones_fr", [1, 128], F32)
        self.sc_fr = place("sc_fr", [1, 128], F32)
        self.s1_sb = place("s1_sb", [128, NHC], F32)
        self.s2_sb = place("s2_sb", [128, NHC], F32)
        self.sh1_sb = place("sh1_sb", [128, NHC], F32)
        self.sh2_sb = place("sh2_sb", [128, NHC], F32)
        self.s3_sb = place("s3_sb", [128, NCH], F32)
        self.bq_sb = place("bq_sb", [128, 2], F32)
        self.bk_sb = place("bk_sb", [128, 2], F32)
        self.bv_sb = place("bv_sb", [128, 256], F32)
        self.bo_sb = place("bo_sb", [128, 8], F32)
        rs = off[0]
        self.mrow = place("mrow", [1, TT], F32)
        self.qrow = place("qrow", [1, TT], F32)
        self.m2row = place("m2row", [1, TT], F32)
        self.varrow = place("varrow", [1, TT], F32)
        self.lnrow = place("lnrow", [1, TT], F32)
        self.arow = place("arow", [1, TT], F32)
        self.brow = place("brow", [1, TT], F32)
        self.Xsq = [place("Xsq0", [128, TT], BF16), place("Xsq1", [128, TT], BF16)]
        self.tmpLN = place("tmpLN", [128, TT], F32)
        self.gtmp = [place("gt0", [128, TT], BF16), place("gt1", [128, TT], BF16)]
        sz_ln = off[0] - rs
        self.den_sb = place("den_sb", [1, QT], F32, at=rs)
        self.rden = place("rden", [1, QT], F32, at=rs + 4096)
        self.lnrow2 = place("lnrow2", [1, QT], F32, at=rs + 12288)
        self.rb_sb = place("rb_sb", [64, QT], F32, at=rs + 8192)
        off[0] = rs + max(sz_ln, 12288)
        assert off[0] <= SZ, f"sbuf over budget: {off[0]}"

    def ps(self, i, half=None):
        if half is None:
            return self.PS[i][:, 0:1024]
        return self.PS[i][:, half * 512:(half + 1) * 512]

    def p_prologue(self):
        E = self.emit
        self.mX = self.dma("sp", self.X[:, :], self.x_in[:, :], self.dX)
        for dst, src in ((self.maskA, self.maskA_in), (self.maskW, self.maskW_in),
                 (self.maskA8, self.maskA8_in),
                         (self.alibi, self.alibi_in), (self.s1_sb, self.s1c),
                         (self.s2_sb, self.s2c), (self.sh1_sb, self.sh1c),
                         (self.sh2_sb, self.sh2c), (self.s3_sb, self.s3c),
                         (self.bq_sb, self.bqc), (self.bk_sb, self.bkc),
                         (self.bv_sb, self.bvc), (self.bo_sb, self.boc)):
            self.mConst = self.dma("sp", dst[:, :], src[:, :], self.dX)
        self.mOnes = E("dve", lambda e: e.memset(self.ones_bf[:, :], 1.0), inc=self.cDVE)
        self.mZeroP = E("dve", lambda e: e.memset(self.zeroP[:, :], 0.0), inc=self.cDVE)
        E("dve", lambda e: e.memset(self.onesKb[:, :], 1.0))
        E("dve", lambda e: e.memset(self.ones_fr[:, :], 1.0))
        E("dve", lambda e: e.memset(self.sc_fr[:, :], 2.0 ** -48))
        self.mOnesK = E("dve", lambda e: e.memset(self.onesK[:, :], 1.0), inc=self.cDVE)

    def ln_reset(self):
        self.prev_bc = None
        self.prev_stats = None
        self.msq_guard = {0: None, 1: None}

    def ln_tile(self, tt, xn, waits_dst=()):
        E = self.emit
        c0 = tt * TT
        w0 = [self.mX, self.mOnesK, self.prev_bc]
        pairs = [(self.onesK[:, :], self.X[:, j * TOK + c0: j * TOK + c0 + TT])
                 for j in range(NCH)]
        mmean = self.mm_group(pairs, self.ps(2, 0)[0:1, :], waits_first=w0)

        mmq = None
        for j in range(NCH):
            sb = self.Xsq[j % 2]
            wsq = [self.mX, self.prev_bc, self.msq_guard[j % 2]]
            sqm = E("act", (lambda e, sb=sb, j=j, c0=c0:
                            e.activation(sb[:, :],
                                         self.X[:, j * TOK + c0: j * TOK + c0 + TT],
                                         AF.Square)),
                    waits=wsq, inc=self.cACT)
            wmm = [sqm]
            if j == 0:
                wmm.append(self.prev_stats)
            mmq = E("te", (lambda e, j=j, sb=sb:
                           e.matmul(self.ps(2, 1)[0:1, :], self.onesKb[:, :], sb[:, :],
                                    start=(j == 0), stop=(j == NCH - 1))),
                    waits=wmm, inc=self.cPE)
            self.msq_guard[j % 2] = mmq

        E("dve", lambda e: e.tensor_scalar(self.mrow[:, :], self.ps(2, 0)[0:1, :],
                                           1.0 / D, None, ALU.mult), waits=[mmean])
        E("dve", lambda e: e.tensor_scalar(self.qrow[:, :], self.ps(2, 1)[0:1, :],
                                           1.0 / D, EPS, ALU.mult, ALU.add), waits=[mmq])
        E("dve", lambda e: e.tensor_mul(self.m2row[:, :], self.mrow[:, :], self.mrow[:, :]))
        mvar = E("dve", lambda e: e.tensor_sub(self.varrow[:, :], self.qrow[:, :],
                                               self.m2row[:, :]), inc=self.cDVE)
        self.prev_stats = mvar
        mln = E("act", lambda e: e.activation(self.lnrow[:, :], self.varrow[:, :], AF.Ln),
                waits=[mvar, self.prev_bc], inc=self.cACT)
        mrstd = E("act", lambda e: e.activation(self.arow[:, :], self.lnrow[:, :], AF.Exp,
                                                scale=-0.5), waits=[mln], inc=self.cACT)
        mbrow = E("dve", lambda e: e.scalar_tensor_tensor(self.brow[:, :], self.mrow[:, :],
                                                          -1.0, self.arow[:, :],
                                                          ALU.mult, ALU.mult),
                  waits=[mrstd], inc=self.cDVE)
        mbcA = E("te", lambda e: e.matmul(self.ps(3, 0), self.ones_fr[:, :], self.arow[:, :],
                                          start=True, stop=True),
                 waits=[mrstd, self.mOnes], inc=self.cPE)
        mbcB = E("te", lambda e: e.matmul(self.ps(3, 1), self.ones_fr[:, :], self.brow[:, :],
                                          start=True, stop=True),
                 waits=[mbrow], inc=self.cPE)
        mapply = None
        for j in range(NCH):
            w1 = [mbcA] + (list(waits_dst) if j == 0 else [])
            E("dve", (lambda e, j=j, c0=c0:
                      e.tensor_mul(self.tmpLN[:, :],
                                   self.X[:, j * TOK + c0: j * TOK + c0 + TT],
                                   self.ps(3, 0))), waits=w1)
            mapply = E("dve", (lambda e, j=j, xn=xn:
                               e.tensor_add(xn[:, j * TT:(j + 1) * TT], self.tmpLN[:, :],
                                            self.ps(3, 1))),
                       waits=[mbcB] if j == 0 else (), inc=self.cDVE)
        self.prev_bc = mapply
        return mapply

    def mk_mg_load(self, buf, tt, jr, ap_sb):
        def fn(e):
            if getattr(self, "_t0_sv", None) is None:
                pid = e.partition_id()
                self._t0_sv = (pid % 4) * TOK
            base = 128 * jr * (2 * S) + tt * TT
            ap_in = bass.AP(self.m_out, self._t0_sv + base, [[2 * S, 128], [S, 2], [1, TT]])
            return e.dma_start(out=ap_sb, in_=ap_in)
        return fn

    def p_attn(self, a):
        PFX = globals().get("ATTN_PREFIX", 6)
        E = self.emit
        self.join()
        self.ln_reset()
        if PFX < 1:
            return

        store_marks = {}
        for tt in range(NTT):
            xn = self.xn_t[tt % 2]
            mapply = self.ln_tile(tt, xn, waits_dst=[store_marks.get(tt - 2)])
            ap_out = bass.AP(self.ag_in, tt * TT, [[NCH * TOK, 128], [TOK, NCH], [1, TT]])
            ap_in = bass.AP(xn, 0, [[NCH * TT, 128], [TT, NCH], [1, TT]])
            store_marks[tt] = self.dma("sp", ap_out, ap_in, self.dS, waits=[mapply])

        if PFX < 2:
            return
        agch = self.cAG[0 if a else 2]
        E("gp", lambda e: e.collective_compute(
            "AllGather", ALU.bypass, replica_groups=[[0, 1, 2, 3], [4, 5, 6, 7]],
            ins=[self.ag_in.ap().opt()], outs=[self.ag_out.ap().opt()]),
            waits=[store_marks[NTT - 1]], inc=agch, k=1)
        mAG1 = (agch, 1)
        if PFX < 3:
            return

        qsrc, ksrc, vsrc = ((self.qwa, self.kwa, self.vwa) if a
                            else (self.qwb, self.kwb, self.vwb))
        wq = self.dma("gp", self.qw_sb[:, :], qsrc[:, :], self.dW)
        wk = self.dma("gp", self.kw_sb[:, :], ksrc[:, :], self.dW)
        wv = self.dma("gp", self.vw_sb[:, :], vsrc[:, :], self.dW)

        vap = bass.AP(self.V_sb, 64, [[64 * 260, 128], [260, 64], [65, 4], [1, 1]])
        mVones = E("dve", lambda e: e.memset(vap, 1.0), inc=self.cDVE)

        kv_load = {}
        last_mm = {}
        kevac = {}
        vevac = {}
        qpm = [None]

        def kv_tile(t16):
            buf = t16 % 2
            rank, tloc = t16 // 4, (t16 % 4) * TT
            ap_in = bass.AP(self.ag_out, rank * 128 * (NCH * TOK) + tloc,
                            [[NCH * TOK, 128], [TOK, NCH], [1, TT]])
            ap_sb = bass.AP(self.xq[buf], 0, [[NCH * TT, 128], [TT, NCH], [1, TT]])
            kv_load[t16] = self.dma("sp", ap_sb, ap_in, self.dX,
                                    waits=[mAG1, last_mm.get(t16 - 2), qpm[0]])
            for fc in range(2):
                slot = (0, fc)
                pd = self.ps(*slot)
                pairs = [(self.kw_sb[:, fc * D + j * 128: fc * D + j * 128 + 128],
                          self.xq[buf][:, j * TT:(j + 1) * TT]) for j in range(NCH)]
                mm = self.mm_group(pairs, pd,
                                   waits_first=[kv_load[t16], wk, self.sfree.get(slot)])
                kdst = self.K_sb[:, fc * S + t16 * TT: fc * S + t16 * TT + TT]
                if a:
                    kevac[(t16, fc)] = E("act", (lambda e, kdst=kdst, pd=pd:
                                                 e.copy(kdst, pd)),
                                         waits=[mm], inc=self.cACT)
                else:
                    kevac[(t16, fc)] = E("dve", (lambda e, kdst=kdst, pd=pd, fc=fc:
                                                 e.tensor_scalar(kdst, pd,
                                                                 self.bk_sb[:, fc:fc + 1],
                                                                 None, ALU.add)),
                                         waits=[mm], inc=self.cDVE)
                self.sfree[slot] = kevac[(t16, fc)]
            for tc in range(4):
                slot = (1, tc % 2)
                pd = self.ps(*slot)[:, 0:256]
                pairs = [(self.xq[buf][:, j * TT + tc * 128: j * TT + tc * 128 + 128],
                          self.vw_sb[:, j * 256:(j + 1) * 256]) for j in range(NCH)]
                mm = self.mm_group(pairs, pd,
                                   waits_first=[kv_load[t16], wv, self.sfree.get(slot)])
                last_mm[t16] = mm
                vbase = (t16 * 4 + tc) * 260
                out_ap = bass.AP(self.V_sb, vbase, [[64 * 260, 128], [65, 4], [1, 64]])
                in_ap = bass.AP(self.PS[slot[0]], slot[1] * 512,
                                [[1024, 128], [64, 4], [1, 64]])
                if a:
                    mv = E("act", (lambda e, o=out_ap, i=in_ap: e.copy(o, i)),
                           waits=[mm, mVones], inc=self.cACT)
                else:
                    bv = bass.AP(self.bv_sb, 0, [[256, 128], [64, 4], [1, 64]])
                    mv = E("dve", (lambda e, o=out_ap, i=in_ap, bv=bv:
                                   e.scalar_tensor_tensor(o, i, 1.0, bv,
                                                          ALU.mult, ALU.add)),
                           waits=[mm, mVones], inc=self.cDVE)
                vevac[(t16, tc)] = mv
                self.sfree[slot] = mv

        if PFX < 4:
            return
        ln_tail = self.cDVE.now()
        self.accfree = {0: ln_tail, 1: ln_tail}
        self.sbankfree = {0: None, 1: None}
        prev_merge = {}
        prev_store = None
        for qt in range(NQT):
            kv_tile(2 * qt)
            kv_tile(2 * qt + 1)
            mQ = []
            for fc in range(2):
                for qh in range(2):
                    slot = (fc, qh)
                    pd = self.ps(*slot)
                    pairs = [(self.qw_sb[:, fc * D + j * 128: fc * D + j * 128 + 128],
                              self.xq[qh][:, j * TT:(j + 1) * TT])
                             for j in range(NCH)]
                    mm = self.mm_group(pairs, pd,
                                       waits_first=[kv_load[2 * qt + qh], wq,
                                                    self.sfree.get(slot),
                                                    self.sbankfree[fc]])
                    qdst = self.Q_sb[:, fc * QT + qh * TT: fc * QT + qh * TT + TT]
                    if a:
                        mqe = E("act", (lambda e, qdst=qdst, pd=pd: e.copy(qdst, pd)),
                                waits=[mm], inc=self.cACT)
                    else:
                        mqe = E("dve", (lambda e, qdst=qdst, pd=pd, fc=fc:
                                        e.tensor_scalar(qdst, pd,
                                                        self.bq_sb[:, fc:fc + 1],
                                                        None, ALU.add)),
                                waits=[mm], inc=self.cDVE)
                    mQ.append(mqe)
                    self.sfree[slot] = mqe
            qpm[0] = self.cPE.now()

            for hp in range(2):
                if a:
                    self.core_a(qt, hp, mQ, kevac, vevac, prev_merge, prev_store)
                else:
                    self.core_b(qt, hp, mQ, kevac, vevac, prev_merge, prev_store)

            wl = [prev_merge[(hp, par)] for hp in range(2) for par in range(2)]
            ap_out = bass.AP(self.m_in, qt * QT, [[2 * S, 128], [S, 2], [1, QT]])
            ap_in2 = bass.AP(self.merged, 0, [[2 * QT, 128], [QT, 2], [1, QT]])
            prev_store = self.dma("sp", ap_out, ap_in2, self.dS, waits=wl)

        if PFX < 5:
            return
        agch2 = self.cAG[1 if a else 3]
        E("gp", lambda e: e.collective_compute(
            "AllGather", ALU.bypass, replica_groups=[[0, 1, 2, 3], [4, 5, 6, 7]],
            ins=[self.m_in.ap().opt()], outs=[self.m_out.ap().opt()]),
            waits=[prev_store], inc=agch2, k=1)
        mAG2 = (agch2, 1)
        if PFX < 6:
            return

        self.strip_gate = [self.dS.now(), self.cDVE.now(), self.cACT.now(),
                           self.cPE.now()]
        osrc = self.owa if a else self.owb
        mow = self.dma("gp", self.ow_sb[:, :], osrc[:, :], self.dW,
                       waits=[self.cPE.now()])

        prev_oproj = {-2: None, -1: None}
        for tt in range(NTT):
            buf = tt % 2
            wl = [mAG2, prev_oproj.get(tt - 2)]
            if tt == 0:
                wl.append(self.cPE.now())
            mload = None
            for jr in range(4):
                ap_sb = bass.AP(self.mg[buf], jr * 1024,
                                [[NCH * TT, 128], [TT, 2], [1, TT]])
                mload = self.emit("sp", self.mk_mg_load(buf, tt, jr, ap_sb),
                                  waits=wl if jr == 0 else (), inc=self.dX, k=16)
            for oc in range(NCH):
                slot = ((oc // 2) % 2, oc % 2)
                pd = self.ps(*slot)
                pairs = [(self.ow_sb[:, oc * D + j * 128: oc * D + j * 128 + 128],
                          self.mg[buf][:, j * TT:(j + 1) * TT]) for j in range(NCH)]
                mm = self.mm_group(pairs, pd,
                                   waits_first=[mload, mow, self.sfree.get(slot)])
                xap = self.X[:, oc * TOK + tt * TT: oc * TOK + tt * TT + TT]
                if a:
                    mv = E("dve", (lambda e, xap=xap, pd=pd: e.tensor_add(xap, xap, pd)),
                           waits=[mm], inc=self.cDVE)
                else:
                    mv = E("dve", (lambda e, xap=xap, pd=pd, oc=oc:
                                   e.scalar_tensor_tensor(xap, pd, self.bo_sb[:, oc:oc + 1],
                                                          xap, ALU.add, ALU.add)),
                           waits=[mm], inc=self.cDVE)
                self.sfree[slot] = mv
            prev_oproj[tt] = self.cPE.now()
        self.join()

    def core_a(self, qt, hp, mQ, kevac, vevac, prev_merge, prev_store):
        E = self.emit
        nkb = qt * 8 + 8
        ACC = (self.PS[2], self.PS[3])
        exp_prev = {0: self.sbankfree[0], 1: self.sbankfree[1]}
        pv_mark = {(p, b): None for p in range(2) for b in range(2)}
        exp_mark = {}
        pend = {}

        def emit_pv(par, kb, dep, ebuf):
            wPV = [dep, vevac.get((kb // 4, kb % 4))]
            if kb == 0:
                wPV.append(self.accfree[par])
            pm = None
            for qh in range(2):
                pm = E("te", (lambda e, par=par, hp=hp, kb=kb, qh=qh, ebuf=ebuf, nkb=nkb:
                              e.matmul(ACC[par][0:65, qh * 512:(qh + 1) * 512],
                                       self.V_sb[:, kb * 260 + (2 * hp + par) * 65:
                                                 kb * 260 + (2 * hp + par) * 65 + 65],
                                       self.P_sb[par][ebuf][:, qh * 512:(qh + 1) * 512],
                                       start=(kb == 0), stop=(kb == nkb - 1))),
                      waits=wPV if qh == 0 else (), inc=self.cPE if qh == 1 else None)
            pv_mark[(par, ebuf)] = pm

        for kb in range(nkb):
            sm = {}
            for par in range(2):
                b0 = par * 64
                sdst = self.PS[par]
                wS = [exp_prev[par], kevac.get((kb // 4, hp))]
                if kb == 0:
                    wS += mQ + [self.sfree.get((par, 0)), self.sfree.get((par, 1))]
                m = None
                for qh in range(2):
                    m = E("te", (lambda e, b0=b0, hp=hp, kb=kb, qh=qh, sdst=sdst:
                                 e.matmul(sdst[:, qh * 512:(qh + 1) * 512],
                                          self.K_sb[b0:b0 + 64,
                                                    hp * S + kb * 128: hp * S + kb * 128 + 128],
                                          self.Q_sb[b0:b0 + 64,
                                                    hp * QT + qh * 512: hp * QT + qh * 512 + 512],
                                          start=True, stop=True, tile_position=(b0, 0))),
                          waits=wS if qh == 0 else (), inc=self.cPE if qh == 1 else None)
                sm[par] = m
            deps = {}
            for par in range(2):
                ebuf = kb % 2
                me = E("act", (lambda e, par=par, ebuf=ebuf:
                               e.activation(self.P_sb[par][ebuf][:, :],
                                            self.PS[par][:, 0:1024],
                                            AF.Exp, scale=1.0 / 32.0)),
                       waits=[sm[par], pv_mark[(par, ebuf)]], inc=self.cACT)
                exp_prev[par] = me
                exp_mark[par] = me
                dep = me
                d = kb - qt * 8
                if d >= 0:
                    dep = E("dve", (lambda e, par=par, ebuf=ebuf, d=d:
                                    e.tensor_mul(self.P_sb[par][ebuf][:, :],
                                                 self.P_sb[par][ebuf][:, :],
                                                 self.maskA[:, 896 - 128 * d:
                                                            896 - 128 * d + 1024])),
                            waits=[me], inc=self.cDVE)
                deps[par] = dep
            for par in range(2):
                if par in pend:
                    emit_pv(par, *pend[par])
                pend[par] = (kb, deps[par], kb % 2)
        for par in range(2):
            emit_pv(par, *pend[par])
        pend.clear()
        for par in range(2):
            self.normalize(hp, par, pv_mark, exp_mark, prev_merge, prev_store,
                           act_recip=True)

    def core_b(self, qt, hp, mQ, kevac, vevac, prev_merge, prev_store):
        E = self.emit
        ACC = (self.PS[2], self.PS[3])
        exp_prev = {0: self.sbankfree[0], 1: self.sbankfree[1]}
        pv_mark = {(p, b): None for p in range(2) for b in range(2)}
        exp_mark = {}
        deltas = (-256, -128, 0)
        pend = {}

        for par in range(2):
            wz = [self.accfree[par], self.mZeroP]
            for qh in range(2):
                E("te", (lambda e, par=par, qh=qh:
                         e.matmul(ACC[par][0:65, qh * 512:(qh + 1) * 512],
                                  self.V_sb[:, 0:65], self.zeroP[:, :],
                                  start=True, stop=False, skip_group_check=True)),
                  waits=wz if qh == 0 else ())

        def emit_pv(par, di, dep, subs, dd):
            pm = None
            first = True
            for s in subs:
                kb = (qt * QT + s * 128 + dd) // 128
                wPV = ([dep] if first else []) + [vevac.get((kb // 4, kb % 4))]
                first = False
                pm = E("te", (lambda e, par=par, hp=hp, kb=kb, s=s, di=di,
                              sp=(di == 2 and (s % 4 == 3)):
                              e.matmul(ACC[par][0:65, s * 128: s * 128 + 128],
                                       self.V_sb[:, kb * 260 + (2 * hp + par) * 65:
                                                 kb * 260 + (2 * hp + par) * 65 + 65],
                                       self.P_sb[par][di % 2][:, s * 128: s * 128 + 128],
                                       start=False, stop=sp, skip_group_check=True)),
                      waits=wPV, inc=self.cPE if s == subs[-1] else None)
            pv_mark[(par, di % 2)] = pm

        for di, dd in enumerate(deltas):
            subs = [s for s in range(8) if qt * QT + s * 128 + dd >= 0]
            s0 = subs[0]
            sm = {}
            for par in range(2):
                b0 = par * 64
                wS0 = [exp_prev[par], kevac.get(((qt * QT + s0 * 128 + dd) // 512, hp))]
                if di == 0:
                    wS0 += mQ + [self.sfree.get((par, 0)), self.sfree.get((par, 1))]
                m = None
                for s in subs:
                    kb = (qt * QT + s * 128 + dd) // 128
                    ws = wS0 if s == s0 else [kevac.get((kb // 4, hp))]
                    m = E("te", (lambda e, b0=b0, hp=hp, kb=kb, s=s, par=par:
                                 e.matmul(self.PS[par][:, s * 128: s * 128 + 128],
                                          self.K_sb[b0:b0 + 64,
                                                    hp * S + kb * 128: hp * S + kb * 128 + 128],
                                          self.Q_sb[b0:b0 + 64,
                                                    hp * QT + s * 128: hp * QT + s * 128 + 128],
                                          start=True, stop=True, tile_position=(b0, 0))),
                          waits=ws, inc=self.cPE if s == subs[-1] else None)
                sm[par] = m
            deps = {}
            for par in range(2):
                me = E("act", (lambda e, par=par, di=di, s0=s0:
                               e.activation(self.P_sb[par][di % 2][:, s0 * 128:1024],
                                            self.PS[par][:, s0 * 128:1024],
                                            AF.Exp, scale=0.125,
                                            bias=self.alibi[:, di:di + 1])),
                       waits=[sm[par], pv_mark[(par, di % 2)], self.mConst],
                       inc=self.cACT)
                exp_prev[par] = me
                exp_mark[par] = me
                dep = me
                if dd == 0:
                    dep = E("dve", (lambda e, par=par, di=di, s0=s0:
                                    e.tensor_mul(self.P_sb[par][di % 2][:, s0 * 128:1024],
                                                 self.P_sb[par][di % 2][:, s0 * 128:1024],
                                                 self.maskA8[:, s0 * 128:1024])),
                            waits=[me], inc=self.cDVE)
                elif dd == -256:
                    dep = E("dve", (lambda e, par=par, di=di, s0=s0:
                                    e.tensor_mul(self.P_sb[par][di % 2][:, s0 * 128:1024],
                                                 self.P_sb[par][di % 2][:, s0 * 128:1024],
                                                 self.maskW[:, s0 * 128:1024])),
                            waits=[me], inc=self.cDVE)
                deps[par] = dep
            for par in range(2):
                if par in pend:
                    emit_pv(par, *pend[par])
                pend[par] = (di, deps[par], subs, dd)
        for par in range(2):
            emit_pv(par, *pend[par])
        pend.clear()
        for par in range(2):
            self.normalize(hp, par, pv_mark, exp_mark, prev_merge, prev_store,
                           act_recip=False)

    def normalize(self, hp, par, pv_mark, exp_mark, prev_merge, prev_store,
                  act_recip=True):
        E = self.emit
        ACC = self.PS[2 + par]
        wpv = [pv_mark.get((par, 0)), pv_mark.get((par, 1))]
        if act_recip:
            mden = E("dve", lambda e: e.tensor_scalar(self.den_sb[:, :],
                                                      ACC[64:65, 0:1024],
                                                      2.0 ** -48, None, ALU.mult),
                     waits=wpv, inc=self.cDVE)
            mln0 = E("act", lambda e: e.activation(self.lnrow2[:, :], self.den_sb[:, :],
                                                   AF.Ln),
                     waits=[mden], inc=self.cACT)
            mrb0 = E("act", lambda e: e.activation(self.rden[:, :], self.lnrow2[:, :],
                                                   AF.Exp, scale=-1.0),
                     waits=[mln0], inc=self.cACT)
        else:
            mden = E("dve", lambda e: e.tensor_copy(self.den_sb[:, :],
                                                    ACC[64:65, 0:1024]),
                     waits=wpv, inc=self.cDVE)
            mrb0 = E("dve", lambda e: e.reciprocal(self.rden[:, :], self.den_sb[:, :]),
                     inc=self.cDVE)
        bm = None
        for qh in range(2):
            blhs = self.sc_fr if act_recip else self.ones_fr
            bm = E("te", (lambda e, par=par, qh=qh, blhs=blhs:
                          e.matmul(ACC[64:128, qh * 512:(qh + 1) * 512],
                                   blhs[0:1, 0:64],
                                   self.rden[0:1, qh * 512:(qh + 1) * 512],
                                   start=True, stop=True, skip_group_check=True,
                                   tile_position=(0, 64))),
                   waits=[mrb0, self.mOnesK] if qh == 0 else (),
                   inc=self.cPE if qh == 1 else None)
        mrb = E("dve", (lambda e, par=par:
                        e.tensor_copy(self.rb_sb[:, :], ACC[64:128, 0:1024])),
                waits=[bm], inc=self.cDVE)
        self.sbankfree[par] = exp_mark[par]
        mmg = E("dve", (lambda e, par=par, hp=hp:
                        e.tensor_mul(self.merged[par * 64: par * 64 + 64,
                                                 hp * QT:(hp + 1) * QT],
                                     ACC[0:64, 0:1024], self.rb_sb[:, :])),
                waits=[mrb, prev_store], inc=self.cDVE)
        prev_merge[(hp, par)] = mmg
        self.accfree[par] = mmg

    def p_ffn(self):
        E = self.emit
        self.join()
        self.ln_reset()
        if not hasattr(self, "ffn_pass"):
            self.ffn_pass = 0
        start_gate = getattr(self, "strip_gate", None) or [
            self.cDVE.now(), self.dS.now(), self.cPE.now()]
        wfree = {0: start_gate, 1: start_gate}
        prev_w3 = None
        prev_a = {-2: None, -1: None}
        for tt in range(NTT):
            xbuf = tt % 2
            xn = self.xn_t[xbuf]
            mapply = self.ln_tile(tt, xn, waits_dst=[prev_a.get(tt - 2)])
            stt_last = None
            stt_m = {-2: None, -1: None}
            for hc in range(NHC):
                wb = self.wstrip[hc % 2]
                wf = wfree[hc % 2] if isinstance(wfree[hc % 2], list) else [wfree[hc % 2]]
                m1 = self.wload(self.w1s, self.w1c, hc, D, wb[:, 0:1024], wf)
                m2 = self.wload(self.w2s, self.w2c, hc, D, wb[:, 1024:2048], ())
                slotA = (hc % 2, 0)
                slotB = (hc % 2, 1)
                pdA, pdB = self.ps(*slotA), self.ps(*slotB)
                pairs = [(wb[:, j * 128:(j + 1) * 128], xn[:, j * TT:(j + 1) * TT])
                         for j in range(NCH)]
                mmA = self.mm_group(pairs, pdA,
                                    waits_first=[m1, mapply, self.sfree.get(slotA)])
                msilu = E("act", (lambda e, hc=hc, pdA=pdA:
                                  e.activation(self.gtmp[hc % 2][:, :], pdA, AF.Silu,
                                               scale=self.s1_sb[:, hc:hc + 1])),
                          waits=[mmA, stt_m[hc - 2]], inc=self.cACT)
                self.sfree[slotA] = msilu
                pairs = [(wb[:, 1024 + j * 128: 1024 + (j + 1) * 128],
                          xn[:, j * TT:(j + 1) * TT]) for j in range(NCH)]
                mmB = self.mm_group(pairs, pdB, waits_first=[m2, self.sfree.get(slotB)])
                wstt = [mmB, msilu]
                if hc == 0:
                    wstt.append(prev_w3)
                stt_last = E("dve", (lambda e, hc=hc, pdB=pdB:
                                     e.scalar_tensor_tensor(
                                         self.h0[:, hc * TT:(hc + 1) * TT], pdB,
                                         self.s2_sb[:, hc:hc + 1],
                                         self.gtmp[hc % 2][:, :], ALU.mult, ALU.mult)),
                              waits=wstt, inc=self.cDVE)
                stt_m[hc] = stt_last
                self.sfree[slotB] = stt_last
                wfree[hc % 2] = mmB
            prev_a[tt] = self.cPE.now()
            g1_last = None
            for hc in range(NHC):
                wb = self.wstrip[hc % 2]
                wf = wfree[hc % 2] if isinstance(wfree[hc % 2], list) else [wfree[hc % 2]]
                mW = self.wload(self.h1s, self.h1c, hc, DH, wb[:, :], wf)
                slot = ((hc % 4) // 2, hc % 2)
                pd = self.ps(*slot)
                pairs = [(wb[:, j * 128:(j + 1) * 128], self.h0[:, j * TT:(j + 1) * TT])
                         for j in range(NHC)]
                mm = self.mm_group(pairs, pd,
                                   waits_first=[mW, stt_last, self.sfree.get(slot)])
                g1_last = E("act", (lambda e, hc=hc, pd=pd:
                                    e.activation(self.g1[:, hc * TT:(hc + 1) * TT], pd,
                                                 AF.Gelu, scale=self.sh1_sb[:, hc:hc + 1])),
                            waits=[mm], inc=self.cACT)
                self.sfree[slot] = g1_last
                wfree[hc % 2] = mm
            g2_last = None
            for hc in range(NHC):
                wb = self.wstrip[hc % 2]
                wf = wfree[hc % 2] if isinstance(wfree[hc % 2], list) else [wfree[hc % 2]]
                mW = self.wload(self.h2s, self.h2c, hc, DH, wb[:, :], wf)
                slot = ((hc % 4) // 2, hc % 2)
                pd = self.ps(*slot)
                pairs = [(wb[:, j * 128:(j + 1) * 128], self.g1[:, j * TT:(j + 1) * TT])
                         for j in range(NHC)]
                mm = self.mm_group(pairs, pd,
                                   waits_first=[mW, g1_last, self.sfree.get(slot)])
                g2_last = E("act", (lambda e, hc=hc, pd=pd:
                                    e.activation(self.h0[:, hc * TT:(hc + 1) * TT], pd,
                                                 AF.Gelu, scale=self.sh2_sb[:, hc:hc + 1])),
                            waits=[mm], inc=self.cACT)
                self.sfree[slot] = g2_last
                wfree[hc % 2] = mm
            for oc in range(NCH):
                wb = self.wstrip[oc % 2]
                wf = wfree[oc % 2] if isinstance(wfree[oc % 2], list) else [wfree[oc % 2]]
                mW = self.wload(self.w3s, self.w3c, oc, DH, wb[:, :], wf)
                slot = ((oc % 4) // 2, oc % 2)
                pd = self.ps(*slot)
                pairs = [(wb[:, j * 128:(j + 1) * 128], self.h0[:, j * TT:(j + 1) * TT])
                         for j in range(NHC)]
                mm = self.mm_group(pairs, pd,
                                   waits_first=[mW, g2_last, self.sfree.get(slot)])
                xap = self.X[:, oc * TOK + tt * TT: oc * TOK + tt * TT + TT]
                mv = E("dve", (lambda e, xap=xap, pd=pd, oc=oc:
                               e.scalar_tensor_tensor(xap, pd, self.s3_sb[:, oc:oc + 1],
                                                      xap, ALU.mult, ALU.add)),
                       waits=[mm], inc=self.cDVE)
                self.sfree[slot] = mv
                wfree[oc % 2] = mm
            prev_w3 = self.cPE.now()
            self.ffn_pass += 1
        self.join()

    def wload(self, src_f32, cache, idx, width, dst_ap, wf):
        if self.ffn_pass == 0:
            m = self.dma("gp", dst_ap,
                         bass.AP(src_f32, idx * 128 * width, [[width, 128], [1, width]]),
                         self.dW, waits=wf)
            ms = self.dma("sp", bass.AP(cache, idx * 128 * width,
                                        [[width, 128], [1, width]]),
                          dst_ap, self.dS, waits=[m])
            if not hasattr(self, "cache_stores"):
                self.cache_stores = {}
            self.cache_stores[(cache.name, idx)] = ms
            return ms
        m = self.dma("gp", dst_ap,
                     bass.AP(cache, idx * 128 * width, [[width, 128], [1, width]]),
                     self.dW,
                     waits=list(wf) + ([self.cache_stores.get((cache.name, idx))]
                                       if self.ffn_pass == 1 else []))
        return m

    def p_epilogue(self):
        self.join()
        m = self.dma("sp", self.y_out[:, :], self.X[:, :], self.dS)
        self.raw_wait("sp", [m])



_CACHE = {}


def _perm_a():
    p = np.arange(1024)
    return (p % 64) * 16 + p // 64


def _lhsT_flat(wT):
    K, M = wT.shape
    s = wT.reshape(K // 128, 128, M // 128, 128).transpose(2, 1, 0, 3)
    return np.ascontiguousarray(
        s.reshape(M // 128, 128, K).transpose(1, 0, 2).reshape(128, (M // 128) * K))


def _lhsT_strips(wT):
    K, M = wT.shape
    s = wT.reshape(K // 128, 128, M // 128, 128).transpose(2, 1, 0, 3)
    return np.ascontiguousarray(s.reshape((M // 128) * 128, K))


def _rhs_flat(wT):
    K, M = wT.shape
    return np.ascontiguousarray(
        wT.reshape(K // 128, 128, M).transpose(1, 0, 2).reshape(128, (K // 128) * M))


def _sc(v, nch):
    return np.ascontiguousarray(v.reshape(nch, 128).T)


def _build_in_maps(inputs):
    f32 = np.float32
    perm = _perm_a()
    qw = np.asarray(inputs["L0_attn_a_qw"], f32)[:, perm]
    kw = np.asarray(inputs["L0_attn_a_kw"], f32)[:, perm]
    vw = np.asarray(inputs["L0_attn_a_vw"], f32)[:, perm]
    ow = np.asarray(inputs["L0_attn_a_ow"], f32)[perm, :]
    wqkv = np.asarray(inputs["L0_attn_b_w_qkv"], f32)
    bqkv = np.asarray(inputs["L0_attn_b_b_qkv"], f32)
    wout = np.asarray(inputs["L0_attn_b_w_out"], f32)
    bout = np.asarray(inputs["L0_attn_b_b_out"], f32)
    x = np.asarray(inputs["L0_attn_a_x"], f32)

    owa = _lhsT_flat(ow)
    owb = _lhsT_flat(np.ascontiguousarray(wout.T))
    w1 = _lhsT_strips(np.ascontiguousarray(np.asarray(inputs["L0_ffn_w1_w"], f32).T))
    w2 = _lhsT_strips(np.ascontiguousarray(np.asarray(inputs["L0_ffn_w2_w"], f32).T))
    h1 = _lhsT_strips(np.ascontiguousarray(np.asarray(inputs["L0_ffn_h1_w"], f32).T))
    h2 = _lhsT_strips(np.ascontiguousarray(np.asarray(inputs["L0_ffn_h2_w"], f32).T))
    w3 = _lhsT_strips(np.ascontiguousarray(np.asarray(inputs["L0_ffn_w3_w"], f32).T))
    s1 = _sc(np.asarray(inputs["L0_ffn_w1_s"], f32), NHC)
    s2 = _sc(np.asarray(inputs["L0_ffn_w2_s"], f32), NHC)
    sh1 = _sc(np.asarray(inputs["L0_ffn_h1_s"], f32), NHC)
    sh2 = _sc(np.asarray(inputs["L0_ffn_h2_s"], f32), NHC)
    s3 = _sc(np.asarray(inputs["L0_ffn_w3_s"], f32), NCH)

    kl = np.arange(128)[:, None]
    cl = np.arange(1920)[None, :]
    maskA = (kl <= cl - 896).astype(BFD)
    ql = np.arange(128)[None, :]
    maskW = np.ascontiguousarray(np.tile((kl >= ql), (1, 8))).astype(BFD)
    maskA8 = np.ascontiguousarray(np.tile((kl <= ql), (1, 8))).astype(BFD)
    alibi = np.ascontiguousarray(
        np.stack([np.arange(128, dtype=f32) + (-256 + 128 * i) - 64
                  for i in range(3)], axis=1).astype(f32))

    in_maps = []
    for c in range(8):
        b, g = c // 4, c % 4
        xs = np.ascontiguousarray(
            x[b, g * TOK:(g + 1) * TOK, :].T.reshape(NCH, 128, TOK)
            .transpose(1, 0, 2).reshape(128, NCH * TOK))
        sl = slice(g * 256, (g + 1) * 256)
        m = {
            "x0f": xs,
            "qwa": _lhsT_flat(np.ascontiguousarray(qw[:, sl])),
            "kwa": _lhsT_flat(np.ascontiguousarray(kw[:, sl])),
            "vwa": _rhs_flat(np.ascontiguousarray(vw[:, sl])),
            "owa": owa,
            "qwb": _lhsT_flat(np.ascontiguousarray(wqkv[sl, :].T)),
            "kwb": _lhsT_flat(np.ascontiguousarray(
                wqkv[1024 + g * 256: 1024 + (g + 1) * 256, :].T)),
            "vwb": _rhs_flat(np.ascontiguousarray(
                wqkv[2048 + g * 256: 2048 + (g + 1) * 256, :].T)),
            "owb": owb,
            "bqc": _sc(bqkv[sl], 2),
            "bkc": _sc(bqkv[1024 + g * 256: 1024 + (g + 1) * 256], 2),
            "bvc": np.ascontiguousarray(np.broadcast_to(
                bqkv[2048 + g * 256: 2048 + (g + 1) * 256], (128, 256))).astype(f32),
            "boc": _sc(bout, NCH),
            "w1s": w1, "w2s": w2, "h1s": h1, "h2s": h2, "w3s": w3,
            "s1c": s1, "s2c": s2, "sh1c": sh1, "sh2c": sh2, "s3c": s3,
            "maskA": maskA, "maskW": maskW, "maskA8": maskA8, "alibi": alibi,
        }
        in_maps.append(m)
    return in_maps


def kernel(**inputs):
    if "nc" not in _CACHE:
        _CACHE["nc"] = MK().build()
    nc = _CACHE["nc"]
    in_maps = _build_in_maps(inputs)
    res = run_bass_kernel_spmd(nc, in_maps, core_ids=list(range(8)))
    out = np.zeros((2, S, D), dtype=np.float32)
    for c in range(8):
        b, g = c // 4, c % 4
        y = res.results[c]["y0f"]
        out[b, g * TOK:(g + 1) * TOK, :] = (
            y.reshape(128, NCH, TOK).transpose(1, 0, 2).reshape(D, TOK).T)
    return out


# version 11
# speedup vs baseline: 1.0196x; 1.0196x over previous
import numpy as np
import ml_dtypes

import concourse.bass as bass
import concourse.mybir as mybir
from concourse.aot_env import allow_ici_all_to_all
from concourse.bass_utils import run_bass_kernel_spmd

F32 = mybir.dt.float32
BF16 = mybir.dt.bfloat16
AF = mybir.ActivationFunctionType
ALU = mybir.AluOpType

D = 1024
S = 8192
TOK = 2048
NCH = 8
DH = 4096
NHC = 32
TT = 512
NTT = TOK // TT
QT = 1024
NQT = S // QT
EPS = 1e-5

ALL_PHASES = ("attnA", "ffn1", "attnB", "ffn2")
KEEP_PHASES = ALL_PHASES

BFD = ml_dtypes.bfloat16


class Ch:
    def __init__(self, nc, name, kind):
        self.h = nc.alloc_semaphore(name)
        self.n = 0
        self.kind = kind

    def mark(self, k=1):
        self.n += k
        return (self, self.n)

    def now(self):
        return (self, self.n)


class MK:
    ENGS = ("sp", "te", "act", "dve", "gp")

    def __init__(self):
        self.nc = bass.Bass("TRN2", target_bir_lowering=False, debug=False)
        self.q = {e: [] for e in self.ENGS}
        self.waited = {e: {} for e in self.ENGS}
        self.pending = {e: {} for e in self.ENGS}

    def emit(self, eng, fn, waits=(), inc=None, k=1):
        merged = dict(self.pending[eng])
        self.pending[eng] = {}
        for m in waits:
            if m is None:
                continue
            ch, v = m
            if v > merged.get(ch, 0):
                merged[ch] = v
        wl = []
        for ch, v in merged.items():
            if ch.kind == eng:
                continue
            if self.waited[eng].get(ch, -1) >= v:
                continue
            self.waited[eng][ch] = v
            wl.append((ch.h, v))
        mark = inc.mark(k) if inc is not None else None
        inch = inc.h if inc is not None else None

        def thunk(e, wl=wl, fn=fn, inch=inch, k=k):
            for h, v in wl:
                e.wait_ge(h, v)
            inst = fn(e)
            if inch is not None:
                inst.then_inc(inch, k)

        self.q[eng].append(thunk)
        return mark

    def raw_wait(self, eng, waits):
        wl = []
        for m in waits:
            if m is None:
                continue
            ch, v = m
            wl.append((ch.h, v))

        def thunk(e, wl=wl):
            for h, v in wl:
                e.wait_ge(h, v)

        self.q[eng].append(thunk)

    def join(self, engines=("sp", "te", "act", "dve", "gp")):
        chans = [self.cPE, self.cACT, self.cDVE, self.dX, self.dS]
        tgt = {ch: ch.n for ch in chans if ch.n > 0}
        for e in engines:
            for ch, v in tgt.items():
                if v > self.pending[e].get(ch, 0):
                    self.pending[e][ch] = v

    def dma(self, eng, out_ap, in_ap, ch, waits=()):
        return self.emit(eng, lambda e: e.dma_start(out=out_ap, in_=in_ap),
                         waits=waits, inc=ch, k=16)

    def mm_group(self, pairs, dst, waits_first=(), tile_position=None):
        n = len(pairs)
        mark = None
        for i, (lhsT, rhs) in enumerate(pairs):
            mark = self.emit(
                "te",
                (lambda e, lhsT=lhsT, rhs=rhs, i=i, n=n, dst=dst, tp=tile_position:
                 e.matmul(dst, lhsT, rhs, start=(i == 0), stop=(i == n - 1),
                          tile_position=tp)),
                waits=waits_first if i == 0 else (),
                inc=self.cPE if i == n - 1 else None)
        return mark

    def build(self):
        nc = self.nc
        self.cPE = Ch(nc, "cPE", "te")
        self.cACT = Ch(nc, "cACT", "act")
        self.cDVE = Ch(nc, "cDVE", "dve")
        self.dX = Ch(nc, "dX", "dma")
        self.dS = Ch(nc, "dS", "dma")
        self.dW = Ch(nc, "dW", "dma")
        self.cAG = [Ch(nc, f"cAG{i}", "dma") for i in range(4)]

        self.declare_io()
        self.alloc_sbuf()
        self.PS = [nc.alloc_psum_tensor(f"PS{i}", [128, 1024], F32) for i in range(4)]
        self.sfree = {}
        self.p_prologue()
        if "attnA" in KEEP_PHASES:
            self.p_attn(True)
        if "ffn1" in KEEP_PHASES:
            self.p_ffn()
        if "attnB" in KEEP_PHASES:
            self.p_attn(False)
        if "ffn2" in KEEP_PHASES:
            self.p_ffn()
        self.p_epilogue()

        with nc.Block() as block:
            @block.sync
            def _(e):
                for f in self.q["sp"]:
                    f(e)

            @block.tensor
            def _(e):
                for f in self.q["te"]:
                    f(e)

            @block.scalar
            def _(e):
                for f in self.q["act"]:
                    f(e)

            @block.vector
            def _(e):
                for f in self.q["dve"]:
                    f(e)

            @block.gpsimd
            def _(e):
                for f in self.q["gp"]:
                    f(e)
        return nc

    def declare_io(self):
        nc = self.nc
        dt = nc.dram_tensor
        self.x_in = dt("x0f", [128, NCH * TOK], F32, kind="ExternalInput")
        self.y_out = dt("y0f", [128, NCH * TOK], F32, kind="ExternalOutput")
        self.qwa = dt("qwa", [128, 2 * D], F32, kind="ExternalInput")
        self.kwa = dt("kwa", [128, 2 * D], F32, kind="ExternalInput")
        self.vwa = dt("vwa", [128, 2 * D], F32, kind="ExternalInput")
        self.owa = dt("owa", [128, 8 * D], F32, kind="ExternalInput")
        self.qwb = dt("qwb", [128, 2 * D], F32, kind="ExternalInput")
        self.kwb = dt("kwb", [128, 2 * D], F32, kind="ExternalInput")
        self.vwb = dt("vwb", [128, 2 * D], F32, kind="ExternalInput")
        self.owb = dt("owb", [128, 8 * D], F32, kind="ExternalInput")
        self.bqc = dt("bqc", [128, 2], F32, kind="ExternalInput")
        self.bkc = dt("bkc", [128, 2], F32, kind="ExternalInput")
        self.bvc = dt("bvc", [128, 256], F32, kind="ExternalInput")
        self.boc = dt("boc", [128, 8], F32, kind="ExternalInput")
        self.w1s = dt("w1s", [NHC * 128, D], F32, kind="ExternalInput")
        self.w2s = dt("w2s", [NHC * 128, D], F32, kind="ExternalInput")
        self.h1s = dt("h1s", [NHC * 128, DH], F32, kind="ExternalInput")
        self.h2s = dt("h2s", [NHC * 128, DH], F32, kind="ExternalInput")
        self.w3s = dt("w3s", [NCH * 128, DH], F32, kind="ExternalInput")
        self.s1c = dt("s1c", [128, NHC], F32, kind="ExternalInput")
        self.s2c = dt("s2c", [128, NHC], F32, kind="ExternalInput")
        self.sh1c = dt("sh1c", [128, NHC], F32, kind="ExternalInput")
        self.sh2c = dt("sh2c", [128, NHC], F32, kind="ExternalInput")
        self.s3c = dt("s3c", [128, NCH], F32, kind="ExternalInput")
        self.maskA_in = dt("maskA", [128, 1920], BF16, kind="ExternalInput")
        self.maskW_in = dt("maskW", [128, 1024], BF16, kind="ExternalInput")
        self.maskA8_in = dt("maskA8", [128, 1024], BF16, kind="ExternalInput")
        self.alibi_in = dt("alibi", [128, 3], F32, kind="ExternalInput")
        self.w1c = dt("w1c", [NHC * 128, D], BF16, kind="Internal")
        self.w2c = dt("w2c", [NHC * 128, D], BF16, kind="Internal")
        self.h1c = dt("h1c", [NHC * 128, DH], BF16, kind="Internal")
        self.h2c = dt("h2c", [NHC * 128, DH], BF16, kind="Internal")
        self.w3c = dt("w3c", [NCH * 128, DH], BF16, kind="Internal")
        self.ag_in = dt("ag_in", [128, NCH * TOK], BF16, kind="Internal")
        self.ag_out = dt("ag_out", [512, NCH * TOK], BF16, kind="Internal")
        self.m_in = dt("m_in", [128, 2 * S], BF16, kind="Internal")
        self.m_out = dt("m_out", [512, 2 * S], BF16, kind="Internal")

    def alloc_sbuf(self):
        nc = self.nc
        off = [0]
        SZ = 212448

        def place(name, shape, dtype, at=None):
            per = int(np.prod(shape[1:])) * mybir.dt.size(dtype)
            per = (per + 31) // 32 * 32
            if at is None:
                at = off[0]
                off[0] = at + per
            else:
                off[0] = max(off[0], at + per)
            assert at + per <= SZ, f"{name} overflows arena: {at + per}"
            return nc.alloc_sbuf_tensor_at(name, shape, dtype, offset=self.base + at)

        arena = nc.alloc_sbuf_tensor("arena", [128, SZ], mybir.dt.uint8)
        self.base = nc.lookup_mloc(arena).addr

        self.X = place("X", [128, NCH * TOK], F32)
        r1 = off[0]
        self.K_sb = place("K_sb", [128, 2 * S], BF16)
        self.h0 = place("h0", [128, NHC * TT], BF16, at=r1)
        self.ow_sb = place("ow_sb", [128, 8 * D], BF16, at=r1)
        off[0] = r1 + 33280
        r2 = off[0]
        self.V_sb = place("V_sb", [128, 64 * 260], BF16)
        self.g1 = place("g1", [128, NHC * TT], BF16, at=r2)
        off[0] = r2 + 33280
        r3 = off[0]
        self.xqq = place("xqq", [128, NCH * QT], BF16)
        self.xq = [place("xq0", [128, NCH * TT], BF16, at=r3),
                   place("xq1", [128, NCH * TT], BF16, at=r3 + 8192)]
        self.mg = [place("mg0", [128, NCH * TT], BF16, at=r3),
                   place("mg1", [128, NCH * TT], BF16, at=r3 + 8192)]
        self.xn_t = [place("xn0", [128, NCH * TT], BF16, at=r3),
                     place("xn1", [128, NCH * TT], BF16, at=r3 + 8192)]
        self.wstrip = [place("ws0", [128, NHC * 128], BF16, at=r3 + 16384),
                       place("ws1", [128, NHC * 128], BF16, at=r3 + 24576)]
        self.Q_sb = place("Q_sb", [128, 2 * QT], BF16, at=r3 + 16384)
        self.P_sb = [[place(f"P{p}{b}", [128, QT], BF16, at=r3 + 20480 + (2 * p + b) * 2048)
                      for b in range(2)] for p in range(2)]
        self.merged = place("merged", [128, 2 * QT], BF16, at=r3 + 28672)
        off[0] = r3 + 32768
        self.qw_sb = place("qw_sb", [128, 2 * D], BF16)
        self.kw_sb = place("kw_sb", [128, 2 * D], BF16)
        self.vw_sb = place("vw_sb", [128, 8 * 256], BF16)
        self.maskA = place("maskA", [128, 1920], BF16)
        self.maskW = place("maskW", [128, 1024], BF16)
        self.maskA8 = place("maskA8", [128, 1024], BF16)
        self.zeroP = place("zeroP", [128, 512], BF16)
        self.alibi = place("alibi", [128, 3], F32)
        self.ones_bf = place("ones_bf", [1, 128], BF16)
        self.onesK = place("onesK", [128, 1], F32)
        self.onesKb = place("onesKb", [128, 1], BF16)
        self.ones_fr = place("ones_fr", [1, 128], F32)
        self.sc_fr = place("sc_fr", [1, 128], F32)
        self.s1_sb = place("s1_sb", [128, NHC], F32)
        self.s2_sb = place("s2_sb", [128, NHC], F32)
        self.sh1_sb = place("sh1_sb", [128, NHC], F32)
        self.sh2_sb = place("sh2_sb", [128, NHC], F32)
        self.s3_sb = place("s3_sb", [128, NCH], F32)
        self.bq_sb = place("bq_sb", [128, 2], F32)
        self.bk_sb = place("bk_sb", [128, 2], F32)
        self.bv_sb = place("bv_sb", [128, 256], F32)
        self.bo_sb = place("bo_sb", [128, 8], F32)
        rs = off[0]
        self.mrow = place("mrow", [1, TT], F32)
        self.qrow = place("qrow", [1, TT], F32)
        self.m2row = place("m2row", [1, TT], F32)
        self.varrow = place("varrow", [1, TT], F32)
        self.lnrow = place("lnrow", [1, TT], F32)
        self.arow = place("arow", [1, TT], F32)
        self.brow = place("brow", [1, TT], F32)
        self.Xsq = [place("Xsq0", [128, TT], BF16), place("Xsq1", [128, TT], BF16)]
        self.tmpLN = place("tmpLN", [128, TT], F32)
        self.gtmp = [place("gt0", [128, TT], BF16), place("gt1", [128, TT], BF16)]
        sz_ln = off[0] - rs
        self.den_sb = place("den_sb", [1, QT], F32, at=rs)
        self.rden = place("rden", [1, QT], F32, at=rs + 4096)
        self.lnrow2 = place("lnrow2", [1, QT], F32, at=rs + 12288)
        self.rb_sb = place("rb_sb", [64, QT], F32, at=rs + 8192)
        off[0] = rs + max(sz_ln, 12288)
        assert off[0] <= SZ, f"sbuf over budget: {off[0]}"

    def ps(self, i, half=None):
        if half is None:
            return self.PS[i][:, 0:1024]
        return self.PS[i][:, half * 512:(half + 1) * 512]

    def p_prologue(self):
        E = self.emit
        self.mX = self.dma("sp", self.X[:, :], self.x_in[:, :], self.dX)
        for dst, src in ((self.maskA, self.maskA_in), (self.maskW, self.maskW_in),
                 (self.maskA8, self.maskA8_in),
                         (self.alibi, self.alibi_in), (self.s1_sb, self.s1c),
                         (self.s2_sb, self.s2c), (self.sh1_sb, self.sh1c),
                         (self.sh2_sb, self.sh2c), (self.s3_sb, self.s3c),
                         (self.bq_sb, self.bqc), (self.bk_sb, self.bkc),
                         (self.bv_sb, self.bvc), (self.bo_sb, self.boc)):
            self.mConst = self.dma("sp", dst[:, :], src[:, :], self.dX)
        self.mOnes = E("dve", lambda e: e.memset(self.ones_bf[:, :], 1.0), inc=self.cDVE)
        self.mZeroP = E("dve", lambda e: e.memset(self.zeroP[:, :], 0.0), inc=self.cDVE)
        E("dve", lambda e: e.memset(self.onesKb[:, :], 1.0))
        E("dve", lambda e: e.memset(self.ones_fr[:, :], 1.0))
        E("dve", lambda e: e.memset(self.sc_fr[:, :], 2.0 ** -48))
        self.mOnesK = E("dve", lambda e: e.memset(self.onesK[:, :], 1.0), inc=self.cDVE)

    def ln_reset(self):
        self.prev_bc = None
        self.prev_stats = None
        self.msq_guard = {0: None, 1: None}

    def ln_tile(self, tt, xn, waits_dst=()):
        E = self.emit
        c0 = tt * TT
        w0 = [self.mX, self.mOnesK, self.prev_bc]
        pairs = [(self.onesK[:, :], self.X[:, j * TOK + c0: j * TOK + c0 + TT])
                 for j in range(NCH)]
        mmean = self.mm_group(pairs, self.ps(2, 0)[0:1, :], waits_first=w0)

        mmq = None
        for j in range(NCH):
            sb = self.Xsq[j % 2]
            wsq = [self.mX, self.prev_bc, self.msq_guard[j % 2]]
            sqm = E("act", (lambda e, sb=sb, j=j, c0=c0:
                            e.activation(sb[:, :],
                                         self.X[:, j * TOK + c0: j * TOK + c0 + TT],
                                         AF.Square)),
                    waits=wsq, inc=self.cACT)
            wmm = [sqm]
            if j == 0:
                wmm.append(self.prev_stats)
            mmq = E("te", (lambda e, j=j, sb=sb:
                           e.matmul(self.ps(2, 1)[0:1, :], self.onesKb[:, :], sb[:, :],
                                    start=(j == 0), stop=(j == NCH - 1))),
                    waits=wmm, inc=self.cPE)
            self.msq_guard[j % 2] = mmq

        E("dve", lambda e: e.tensor_scalar(self.mrow[:, :], self.ps(2, 0)[0:1, :],
                                           1.0 / D, None, ALU.mult), waits=[mmean])
        E("dve", lambda e: e.tensor_scalar(self.qrow[:, :], self.ps(2, 1)[0:1, :],
                                           1.0 / D, EPS, ALU.mult, ALU.add), waits=[mmq])
        E("dve", lambda e: e.tensor_mul(self.m2row[:, :], self.mrow[:, :], self.mrow[:, :]))
        mvar = E("dve", lambda e: e.tensor_sub(self.varrow[:, :], self.qrow[:, :],
                                               self.m2row[:, :]), inc=self.cDVE)
        self.prev_stats = mvar
        mln = E("act", lambda e: e.activation(self.lnrow[:, :], self.varrow[:, :], AF.Ln),
                waits=[mvar, self.prev_bc], inc=self.cACT)
        mrstd = E("act", lambda e: e.activation(self.arow[:, :], self.lnrow[:, :], AF.Exp,
                                                scale=-0.5), waits=[mln], inc=self.cACT)
        mbrow = E("dve", lambda e: e.scalar_tensor_tensor(self.brow[:, :], self.mrow[:, :],
                                                          -1.0, self.arow[:, :],
                                                          ALU.mult, ALU.mult),
                  waits=[mrstd], inc=self.cDVE)
        mbcA = E("te", lambda e: e.matmul(self.ps(3, 0), self.ones_fr[:, :], self.arow[:, :],
                                          start=True, stop=True),
                 waits=[mrstd, self.mOnes], inc=self.cPE)
        mbcB = E("te", lambda e: e.matmul(self.ps(3, 1), self.ones_fr[:, :], self.brow[:, :],
                                          start=True, stop=True),
                 waits=[mbrow], inc=self.cPE)
        mapply = None
        for j in range(NCH):
            w1 = [mbcA] + (list(waits_dst) if j == 0 else [])
            E("dve", (lambda e, j=j, c0=c0:
                      e.tensor_mul(self.tmpLN[:, :],
                                   self.X[:, j * TOK + c0: j * TOK + c0 + TT],
                                   self.ps(3, 0))), waits=w1)
            mapply = E("dve", (lambda e, j=j, xn=xn:
                               e.tensor_add(xn[:, j * TT:(j + 1) * TT], self.tmpLN[:, :],
                                            self.ps(3, 1))),
                       waits=[mbcB] if j == 0 else (), inc=self.cDVE)
        self.prev_bc = mapply
        return mapply

    def mk_mg_load(self, buf, tt, jr, ap_sb):
        def fn(e):
            cache = getattr(self, "_t0_sv", None)
            if cache is None:
                cache = self._t0_sv = {}
            key = id(e)
            if key not in cache:
                pid = e.partition_id()
                cache[key] = (pid % 4) * TOK
            base = 128 * jr * (2 * S) + tt * TT
            ap_in = bass.AP(self.m_out, cache[key] + base,
                            [[2 * S, 128], [S, 2], [1, TT]])
            return e.dma_start(out=ap_sb, in_=ap_in)
        return fn

    def p_attn(self, a):
        PFX = globals().get("ATTN_PREFIX", 6)
        E = self.emit
        self.join()
        self.ln_reset()
        if PFX < 1:
            return

        store_marks = {}
        for tt in range(NTT):
            xn = self.xn_t[tt % 2]
            mapply = self.ln_tile(tt, xn, waits_dst=[store_marks.get(tt - 2)])
            ap_out = bass.AP(self.ag_in, tt * TT, [[NCH * TOK, 128], [TOK, NCH], [1, TT]])
            ap_in = bass.AP(xn, 0, [[NCH * TT, 128], [TT, NCH], [1, TT]])
            store_marks[tt] = self.dma("sp", ap_out, ap_in, self.dS, waits=[mapply])

        if PFX < 2:
            return
        agch = self.cAG[0 if a else 2]
        E("gp", lambda e: e.collective_compute(
            "AllGather", ALU.bypass, replica_groups=[[0, 1, 2, 3], [4, 5, 6, 7]],
            ins=[self.ag_in.ap().opt()], outs=[self.ag_out.ap().opt()]),
            waits=[store_marks[NTT - 1]], inc=agch, k=1)
        mAG1 = (agch, 1)
        if PFX < 3:
            return

        qsrc, ksrc, vsrc = ((self.qwa, self.kwa, self.vwa) if a
                            else (self.qwb, self.kwb, self.vwb))
        wq = self.dma("gp", self.qw_sb[:, :], qsrc[:, :], self.dW)
        wk = self.dma("gp", self.kw_sb[:, :], ksrc[:, :], self.dW)
        wv = self.dma("gp", self.vw_sb[:, :], vsrc[:, :], self.dW)

        vap = bass.AP(self.V_sb, 64, [[64 * 260, 128], [260, 64], [65, 4], [1, 1]])
        mVones = E("dve", lambda e: e.memset(vap, 1.0), inc=self.cDVE)

        kv_load = {}
        last_mm = {}
        kevac = {}
        for t16 in range(16):
            buf = t16 % 2
            rank, tloc = t16 // 4, (t16 % 4) * TT
            base = rank * 128 * (NCH * TOK) + tloc
            wl_kv = [mAG1, last_mm.get(t16 - 2)]
            self.dma("sp", bass.AP(self.xq[buf], 0, [[NCH * TT, 128], [TT, 4], [1, TT]]),
                     bass.AP(self.ag_out, base, [[NCH * TOK, 128], [TOK, 4], [1, TT]]),
                     self.dX, waits=wl_kv)
            kv_load[t16] = self.dma(
                "act", bass.AP(self.xq[buf], 4 * TT, [[NCH * TT, 128], [TT, 4], [1, TT]]),
                bass.AP(self.ag_out, base + 4 * TOK, [[NCH * TOK, 128], [TOK, 4], [1, TT]]),
                self.dX, waits=wl_kv)
            for fc in range(2):
                slot = (0, fc)
                pd = self.ps(*slot)
                pairs = [(self.kw_sb[:, fc * D + j * 128: fc * D + j * 128 + 128],
                          self.xq[buf][:, j * TT:(j + 1) * TT]) for j in range(NCH)]
                mm = self.mm_group(pairs, pd,
                                   waits_first=[kv_load[t16], wk, self.sfree.get(slot)])
                kdst = self.K_sb[:, fc * S + t16 * TT: fc * S + t16 * TT + TT]
                if a:
                    kevac[(t16, fc)] = E("act", (lambda e, kdst=kdst, pd=pd:
                                                 e.copy(kdst, pd)),
                                         waits=[mm], inc=self.cACT)
                else:
                    kevac[(t16, fc)] = E("dve", (lambda e, kdst=kdst, pd=pd, fc=fc:
                                                 e.tensor_scalar(kdst, pd,
                                                                 self.bk_sb[:, fc:fc + 1],
                                                                 None, ALU.add)),
                                         waits=[mm], inc=self.cDVE)
                self.sfree[slot] = kevac[(t16, fc)]
            for tc in range(4):
                slot = (1, tc % 2)
                pd = self.ps(*slot)[:, 0:256]
                pairs = [(self.xq[buf][:, j * TT + tc * 128: j * TT + tc * 128 + 128],
                          self.vw_sb[:, j * 256:(j + 1) * 256]) for j in range(NCH)]
                mm = self.mm_group(pairs, pd,
                                   waits_first=[kv_load[t16], wv, self.sfree.get(slot)])
                last_mm[t16] = mm
                vbase = (t16 * 4 + tc) * 260
                out_ap = bass.AP(self.V_sb, vbase, [[64 * 260, 128], [65, 4], [1, 64]])
                in_ap = bass.AP(self.PS[slot[0]], slot[1] * 512, [[1024, 128], [64, 4], [1, 64]])
                if a:
                    mv = E("act", (lambda e, o=out_ap, i=in_ap: e.copy(o, i)),
                           waits=[mm, mVones], inc=self.cACT)
                else:
                    bv = bass.AP(self.bv_sb, 0, [[256, 128], [64, 4], [1, 64]])
                    mv = E("dve", (lambda e, o=out_ap, i=in_ap, bv=bv:
                                   e.scalar_tensor_tensor(o, i, 1.0, bv, ALU.mult, ALU.add)),
                           waits=[mm, mVones], inc=self.cDVE)
                self.sfree[slot] = mv

        if PFX < 4:
            return
        self.join(engines=("te",))
        prev_qproj = self.cPE.now()
        self.accfree = {0: None, 1: None}
        self.sbankfree = {0: None, 1: None}
        prev_merge = {}
        prev_store = None
        for qt in range(NQT):
            rank, tloc = (qt * QT) // TOK, (qt * QT) % TOK
            base = rank * 128 * (NCH * TOK) + tloc
            wl_q = [mAG1, prev_qproj]
            self.dma("sp", bass.AP(self.xqq, 0, [[NCH * QT, 128], [QT, 4], [1, QT]]),
                     bass.AP(self.ag_out, base, [[NCH * TOK, 128], [TOK, 4], [1, QT]]),
                     self.dX, waits=wl_q)
            mload = self.dma(
                "act", bass.AP(self.xqq, 4 * QT, [[NCH * QT, 128], [QT, 4], [1, QT]]),
                bass.AP(self.ag_out, base + 4 * TOK, [[NCH * TOK, 128], [TOK, 4], [1, QT]]),
                self.dX, waits=wl_q)
            mQ = []
            for fc in range(2):
                for qh in range(2):
                    slot = (fc, qh)
                    pd = self.ps(*slot)
                    pairs = [(self.qw_sb[:, fc * D + j * 128: fc * D + j * 128 + 128],
                              self.xqq[:, j * QT + qh * TT: j * QT + qh * TT + TT])
                             for j in range(NCH)]
                    mm = self.mm_group(pairs, pd,
                                       waits_first=[mload, wq, self.sfree.get(slot),
                                                    self.sbankfree[fc]])
                    qdst = self.Q_sb[:, fc * QT + qh * TT: fc * QT + qh * TT + TT]
                    if a:
                        mqe = E("act", (lambda e, qdst=qdst, pd=pd: e.copy(qdst, pd)),
                                waits=[mm], inc=self.cACT)
                    else:
                        mqe = E("dve", (lambda e, qdst=qdst, pd=pd, fc=fc:
                                        e.tensor_scalar(qdst, pd,
                                                        self.bq_sb[:, fc:fc + 1],
                                                        None, ALU.add)),
                                waits=[mm], inc=self.cDVE)
                    mQ.append(mqe)
                    self.sfree[slot] = mqe
            prev_qproj = self.cPE.now()

            for hp in range(2):
                if a:
                    self.core_a(qt, hp, mQ, kevac, prev_merge, prev_store)
                else:
                    self.core_b(qt, hp, mQ, kevac, prev_merge, prev_store)

            wl = [prev_merge[(hp, par)] for hp in range(2) for par in range(2)]
            ap_out = bass.AP(self.m_in, qt * QT, [[2 * S, 128], [S, 2], [1, QT]])
            ap_in2 = bass.AP(self.merged, 0, [[2 * QT, 128], [QT, 2], [1, QT]])
            prev_store = self.dma("sp", ap_out, ap_in2, self.dS, waits=wl)

        if PFX < 5:
            return
        agch2 = self.cAG[1 if a else 3]
        E("gp", lambda e: e.collective_compute(
            "AllGather", ALU.bypass, replica_groups=[[0, 1, 2, 3], [4, 5, 6, 7]],
            ins=[self.m_in.ap().opt()], outs=[self.m_out.ap().opt()]),
            waits=[prev_store], inc=agch2, k=1)
        mAG2 = (agch2, 1)
        if PFX < 6:
            return

        self.strip_gate = [self.dS.now(), self.cDVE.now(), self.cACT.now(),
                           self.cPE.now()]
        osrc = self.owa if a else self.owb
        mow = self.dma("gp", self.ow_sb[:, :], osrc[:, :], self.dW,
                       waits=[self.cPE.now()])

        prev_oproj = {-2: None, -1: None}
        for tt in range(NTT):
            buf = tt % 2
            wl = [mAG2, prev_oproj.get(tt - 2)]
            if tt == 0:
                wl.append(self.cPE.now())
            mload = None
            for jr in range(4):
                ap_sb = bass.AP(self.mg[buf], jr * 1024,
                                [[NCH * TT, 128], [TT, 2], [1, TT]])
                mload = self.emit("sp" if jr % 2 == 0 else "act",
                                  self.mk_mg_load(buf, tt, jr, ap_sb),
                                  waits=wl if jr < 2 else (), inc=self.dX, k=16)
            for oc in range(NCH):
                slot = ((oc // 2) % 2, oc % 2)
                pd = self.ps(*slot)
                pairs = [(self.ow_sb[:, oc * D + j * 128: oc * D + j * 128 + 128],
                          self.mg[buf][:, j * TT:(j + 1) * TT]) for j in range(NCH)]
                mm = self.mm_group(pairs, pd,
                                   waits_first=[mload, mow, self.sfree.get(slot)])
                xap = self.X[:, oc * TOK + tt * TT: oc * TOK + tt * TT + TT]
                if a:
                    mv = E("dve", (lambda e, xap=xap, pd=pd: e.tensor_add(xap, xap, pd)),
                           waits=[mm], inc=self.cDVE)
                else:
                    mv = E("dve", (lambda e, xap=xap, pd=pd, oc=oc:
                                   e.scalar_tensor_tensor(xap, pd, self.bo_sb[:, oc:oc + 1],
                                                          xap, ALU.add, ALU.add)),
                           waits=[mm], inc=self.cDVE)
                self.sfree[slot] = mv
            prev_oproj[tt] = self.cPE.now()
        self.join()

    def core_a(self, qt, hp, mQ, kevac, prev_merge, prev_store):
        E = self.emit
        nkb = qt * 8 + 8
        ACC = (self.PS[2], self.PS[3])
        exp_prev = {0: self.sbankfree[0], 1: self.sbankfree[1]}
        pv_mark = {(p, b): None for p in range(2) for b in range(2)}
        exp_mark = {}
        pend = {}

        def emit_pv(par, kb, dep, ebuf):
            wPV = [dep]
            if kb == 0:
                wPV.append(self.accfree[par])
            pm = None
            for qh in range(2):
                pm = E("te", (lambda e, par=par, hp=hp, kb=kb, qh=qh, ebuf=ebuf, nkb=nkb:
                              e.matmul(ACC[par][0:65, qh * 512:(qh + 1) * 512],
                                       self.V_sb[:, kb * 260 + (2 * hp + par) * 65:
                                                 kb * 260 + (2 * hp + par) * 65 + 65],
                                       self.P_sb[par][ebuf][:, qh * 512:(qh + 1) * 512],
                                       start=(kb == 0), stop=(kb == nkb - 1))),
                      waits=wPV if qh == 0 else (), inc=self.cPE if qh == 1 else None)
            pv_mark[(par, ebuf)] = pm

        for kb in range(nkb):
            sm = {}
            for par in range(2):
                b0 = par * 64
                sdst = self.PS[par]
                wS = [exp_prev[par], kevac.get((kb // 4, hp))]
                if kb == 0:
                    wS += mQ + [self.sfree.get((par, 0)), self.sfree.get((par, 1))]
                m = None
                for qh in range(2):
                    m = E("te", (lambda e, b0=b0, hp=hp, kb=kb, qh=qh, sdst=sdst:
                                 e.matmul(sdst[:, qh * 512:(qh + 1) * 512],
                                          self.K_sb[b0:b0 + 64,
                                                    hp * S + kb * 128: hp * S + kb * 128 + 128],
                                          self.Q_sb[b0:b0 + 64,
                                                    hp * QT + qh * 512: hp * QT + qh * 512 + 512],
                                          start=True, stop=True, tile_position=(b0, 0))),
                          waits=wS if qh == 0 else (), inc=self.cPE if qh == 1 else None)
                sm[par] = m
            deps = {}
            for par in range(2):
                ebuf = kb % 2
                me = E("act", (lambda e, par=par, ebuf=ebuf:
                               e.activation(self.P_sb[par][ebuf][:, :],
                                            self.PS[par][:, 0:1024],
                                            AF.Exp, scale=1.0 / 32.0)),
                       waits=[sm[par], pv_mark[(par, ebuf)]], inc=self.cACT)
                exp_prev[par] = me
                exp_mark[par] = me
                dep = me
                d = kb - qt * 8
                if d >= 0:
                    dep = E("dve", (lambda e, par=par, ebuf=ebuf, d=d:
                                    e.tensor_mul(self.P_sb[par][ebuf][:, :],
                                                 self.P_sb[par][ebuf][:, :],
                                                 self.maskA[:, 896 - 128 * d:
                                                            896 - 128 * d + 1024])),
                            waits=[me], inc=self.cDVE)
                deps[par] = dep
            for par in range(2):
                if par in pend:
                    emit_pv(par, *pend[par])
                pend[par] = (kb, deps[par], kb % 2)
        for par in range(2):
            emit_pv(par, *pend[par])
        pend.clear()
        for par in range(2):
            self.normalize(hp, par, pv_mark, exp_mark, prev_merge, prev_store,
                           act_recip=True)

    def core_b(self, qt, hp, mQ, kevac, prev_merge, prev_store):
        E = self.emit
        ACC = (self.PS[2], self.PS[3])
        exp_prev = {0: self.sbankfree[0], 1: self.sbankfree[1]}
        pv_mark = {(p, b): None for p in range(2) for b in range(2)}
        exp_mark = {}
        deltas = (-256, -128, 0)
        pend = {}

        for par in range(2):
            wz = [self.accfree[par], self.mZeroP]
            for qh in range(2):
                E("te", (lambda e, par=par, qh=qh:
                         e.matmul(ACC[par][0:65, qh * 512:(qh + 1) * 512],
                                  self.V_sb[:, 0:65], self.zeroP[:, :],
                                  start=True, stop=False, skip_group_check=True)),
                  waits=wz if qh == 0 else ())

        def emit_pv(par, di, dep, subs, dd):
            pm = None
            first = True
            for s in subs:
                kb = (qt * QT + s * 128 + dd) // 128
                wPV = [dep] if first else ()
                first = False
                pm = E("te", (lambda e, par=par, hp=hp, kb=kb, s=s, di=di,
                              sp=(di == 2 and (s % 4 == 3)):
                              e.matmul(ACC[par][0:65, s * 128: s * 128 + 128],
                                       self.V_sb[:, kb * 260 + (2 * hp + par) * 65:
                                                 kb * 260 + (2 * hp + par) * 65 + 65],
                                       self.P_sb[par][di % 2][:, s * 128: s * 128 + 128],
                                       start=False, stop=sp, skip_group_check=True)),
                      waits=wPV, inc=self.cPE if s == subs[-1] else None)
            pv_mark[(par, di % 2)] = pm

        for di, dd in enumerate(deltas):
            subs = [s for s in range(8) if qt * QT + s * 128 + dd >= 0]
            s0 = subs[0]
            sm = {}
            for par in range(2):
                b0 = par * 64
                wS0 = [exp_prev[par], kevac.get(((qt * QT + s0 * 128 + dd) // 512, hp))]
                if di == 0:
                    wS0 += mQ + [self.sfree.get((par, 0)), self.sfree.get((par, 1))]
                m = None
                for s in subs:
                    kb = (qt * QT + s * 128 + dd) // 128
                    ws = wS0 if s == s0 else [kevac.get((kb // 4, hp))]
                    m = E("te", (lambda e, b0=b0, hp=hp, kb=kb, s=s, par=par:
                                 e.matmul(self.PS[par][:, s * 128: s * 128 + 128],
                                          self.K_sb[b0:b0 + 64,
                                                    hp * S + kb * 128: hp * S + kb * 128 + 128],
                                          self.Q_sb[b0:b0 + 64,
                                                    hp * QT + s * 128: hp * QT + s * 128 + 128],
                                          start=True, stop=True, tile_position=(b0, 0))),
                          waits=ws, inc=self.cPE if s == subs[-1] else None)
                sm[par] = m
            deps = {}
            for par in range(2):
                me = E("act", (lambda e, par=par, di=di, s0=s0:
                               e.activation(self.P_sb[par][di % 2][:, s0 * 128:1024],
                                            self.PS[par][:, s0 * 128:1024],
                                            AF.Exp, scale=0.125,
                                            bias=self.alibi[:, di:di + 1])),
                       waits=[sm[par], pv_mark[(par, di % 2)], self.mConst],
                       inc=self.cACT)
                exp_prev[par] = me
                exp_mark[par] = me
                dep = me
                if dd == 0:
                    dep = E("dve", (lambda e, par=par, di=di, s0=s0:
                                    e.tensor_mul(self.P_sb[par][di % 2][:, s0 * 128:1024],
                                                 self.P_sb[par][di % 2][:, s0 * 128:1024],
                                                 self.maskA8[:, s0 * 128:1024])),
                            waits=[me], inc=self.cDVE)
                elif dd == -256:
                    dep = E("dve", (lambda e, par=par, di=di, s0=s0:
                                    e.tensor_mul(self.P_sb[par][di % 2][:, s0 * 128:1024],
                                                 self.P_sb[par][di % 2][:, s0 * 128:1024],
                                                 self.maskW[:, s0 * 128:1024])),
                            waits=[me], inc=self.cDVE)
                deps[par] = dep
            for par in range(2):
                if par in pend:
                    emit_pv(par, *pend[par])
                pend[par] = (di, deps[par], subs, dd)
        for par in range(2):
            emit_pv(par, *pend[par])
        pend.clear()
        for par in range(2):
            self.normalize(hp, par, pv_mark, exp_mark, prev_merge, prev_store,
                           act_recip=False)

    def normalize(self, hp, par, pv_mark, exp_mark, prev_merge, prev_store,
                  act_recip=True):
        E = self.emit
        ACC = self.PS[2 + par]
        wpv = [pv_mark.get((par, 0)), pv_mark.get((par, 1))]
        if act_recip:
            mden = E("dve", lambda e: e.tensor_scalar(self.den_sb[:, :],
                                                      ACC[64:65, 0:1024],
                                                      2.0 ** -48, None, ALU.mult),
                     waits=wpv, inc=self.cDVE)
            mln0 = E("act", lambda e: e.activation(self.lnrow2[:, :], self.den_sb[:, :],
                                                   AF.Ln),
                     waits=[mden], inc=self.cACT)
            mrb0 = E("act", lambda e: e.activation(self.rden[:, :], self.lnrow2[:, :],
                                                   AF.Exp, scale=-1.0),
                     waits=[mln0], inc=self.cACT)
        else:
            mrb0 = E("dve", lambda e: e.reciprocal(self.rden[:, :],
                                                   ACC[64:65, 0:1024]),
                     waits=wpv, inc=self.cDVE)
        bm = None
        for qh in range(2):
            blhs = self.sc_fr if act_recip else self.ones_fr
            bm = E("te", (lambda e, par=par, qh=qh, blhs=blhs:
                          e.matmul(ACC[64:128, qh * 512:(qh + 1) * 512],
                                   blhs[0:1, 0:64],
                                   self.rden[0:1, qh * 512:(qh + 1) * 512],
                                   start=True, stop=True, skip_group_check=True,
                                   tile_position=(0, 64))),
                   waits=[mrb0, self.mOnesK] if qh == 0 else (),
                   inc=self.cPE if qh == 1 else None)
        mrb = E("dve", (lambda e, par=par:
                        e.tensor_copy(self.rb_sb[:, :], ACC[64:128, 0:1024])),
                waits=[bm], inc=self.cDVE)
        self.sbankfree[par] = exp_mark[par]
        mmg = E("dve", (lambda e, par=par, hp=hp:
                        e.tensor_mul(self.merged[par * 64: par * 64 + 64,
                                                 hp * QT:(hp + 1) * QT],
                                     ACC[0:64, 0:1024], self.rb_sb[:, :])),
                waits=[mrb, prev_store], inc=self.cDVE)
        prev_merge[(hp, par)] = mmg
        self.accfree[par] = mmg

    def p_ffn(self):
        E = self.emit
        self.join()
        self.ln_reset()
        if not hasattr(self, "ffn_pass"):
            self.ffn_pass = 0
        start_gate = getattr(self, "strip_gate", None) or [
            self.cDVE.now(), self.dS.now(), self.cPE.now()]
        wfree = {0: start_gate, 1: start_gate}
        prev_w3 = None
        prev_a = {-2: None, -1: None}
        for tt in range(NTT):
            xbuf = tt % 2
            xn = self.xn_t[xbuf]
            mapply = self.ln_tile(tt, xn, waits_dst=[prev_a.get(tt - 2)])
            stt_last = None
            stt_m = {-2: None, -1: None}
            for hc in range(NHC):
                wb = self.wstrip[hc % 2]
                wf = wfree[hc % 2] if isinstance(wfree[hc % 2], list) else [wfree[hc % 2]]
                m1 = self.wload(self.w1s, self.w1c, hc, D, wb[:, 0:1024], wf)
                m2 = self.wload(self.w2s, self.w2c, hc, D, wb[:, 1024:2048], ())
                slotA = (hc % 2, 0)
                slotB = (hc % 2, 1)
                pdA, pdB = self.ps(*slotA), self.ps(*slotB)
                pairs = [(wb[:, j * 128:(j + 1) * 128], xn[:, j * TT:(j + 1) * TT])
                         for j in range(NCH)]
                mmA = self.mm_group(pairs, pdA,
                                    waits_first=[m1, mapply, self.sfree.get(slotA)])
                msilu = E("act", (lambda e, hc=hc, pdA=pdA:
                                  e.activation(self.gtmp[hc % 2][:, :], pdA, AF.Silu,
                                               scale=self.s1_sb[:, hc:hc + 1])),
                          waits=[mmA, stt_m[hc - 2]], inc=self.cACT)
                self.sfree[slotA] = msilu
                pairs = [(wb[:, 1024 + j * 128: 1024 + (j + 1) * 128],
                          xn[:, j * TT:(j + 1) * TT]) for j in range(NCH)]
                mmB = self.mm_group(pairs, pdB, waits_first=[m2, self.sfree.get(slotB)])
                wstt = [mmB, msilu]
                if hc == 0:
                    wstt.append(prev_w3)
                stt_last = E("dve", (lambda e, hc=hc, pdB=pdB:
                                     e.scalar_tensor_tensor(
                                         self.h0[:, hc * TT:(hc + 1) * TT], pdB,
                                         self.s2_sb[:, hc:hc + 1],
                                         self.gtmp[hc % 2][:, :], ALU.mult, ALU.mult)),
                              waits=wstt, inc=self.cDVE)
                stt_m[hc] = stt_last
                self.sfree[slotB] = stt_last
                wfree[hc % 2] = mmB
            prev_a[tt] = self.cPE.now()
            g1_last = None
            for hc in range(NHC):
                wb = self.wstrip[hc % 2]
                wf = wfree[hc % 2] if isinstance(wfree[hc % 2], list) else [wfree[hc % 2]]
                mW = self.wload(self.h1s, self.h1c, hc, DH, wb[:, :], wf)
                slot = ((hc % 4) // 2, hc % 2)
                pd = self.ps(*slot)
                pairs = [(wb[:, j * 128:(j + 1) * 128], self.h0[:, j * TT:(j + 1) * TT])
                         for j in range(NHC)]
                mm = self.mm_group(pairs, pd,
                                   waits_first=[mW, stt_last, self.sfree.get(slot)])
                g1_last = E("act", (lambda e, hc=hc, pd=pd:
                                    e.activation(self.g1[:, hc * TT:(hc + 1) * TT], pd,
                                                 AF.Gelu, scale=self.sh1_sb[:, hc:hc + 1])),
                            waits=[mm], inc=self.cACT)
                self.sfree[slot] = g1_last
                wfree[hc % 2] = mm
            g2_last = None
            for hc in range(NHC):
                wb = self.wstrip[hc % 2]
                wf = wfree[hc % 2] if isinstance(wfree[hc % 2], list) else [wfree[hc % 2]]
                mW = self.wload(self.h2s, self.h2c, hc, DH, wb[:, :], wf)
                slot = ((hc % 4) // 2, hc % 2)
                pd = self.ps(*slot)
                pairs = [(wb[:, j * 128:(j + 1) * 128], self.g1[:, j * TT:(j + 1) * TT])
                         for j in range(NHC)]
                mm = self.mm_group(pairs, pd,
                                   waits_first=[mW, g1_last, self.sfree.get(slot)])
                g2_last = E("act", (lambda e, hc=hc, pd=pd:
                                    e.activation(self.h0[:, hc * TT:(hc + 1) * TT], pd,
                                                 AF.Gelu, scale=self.sh2_sb[:, hc:hc + 1])),
                            waits=[mm], inc=self.cACT)
                self.sfree[slot] = g2_last
                wfree[hc % 2] = mm
            for oc in range(NCH):
                wb = self.wstrip[oc % 2]
                wf = wfree[oc % 2] if isinstance(wfree[oc % 2], list) else [wfree[oc % 2]]
                mW = self.wload(self.w3s, self.w3c, oc, DH, wb[:, :], wf)
                slot = ((oc % 4) // 2, oc % 2)
                pd = self.ps(*slot)
                pairs = [(wb[:, j * 128:(j + 1) * 128], self.h0[:, j * TT:(j + 1) * TT])
                         for j in range(NHC)]
                mm = self.mm_group(pairs, pd,
                                   waits_first=[mW, g2_last, self.sfree.get(slot)])
                xap = self.X[:, oc * TOK + tt * TT: oc * TOK + tt * TT + TT]
                mv = E("dve", (lambda e, xap=xap, pd=pd, oc=oc:
                               e.scalar_tensor_tensor(xap, pd, self.s3_sb[:, oc:oc + 1],
                                                      xap, ALU.mult, ALU.add)),
                       waits=[mm], inc=self.cDVE)
                self.sfree[slot] = mv
                wfree[oc % 2] = mm
            prev_w3 = self.cPE.now()
            self.ffn_pass += 1
        self.join()

    def wload(self, src_f32, cache, idx, width, dst_ap, wf):
        if self.ffn_pass == 0:
            m = self.dma("gp", dst_ap,
                         bass.AP(src_f32, idx * 128 * width, [[width, 128], [1, width]]),
                         self.dW, waits=wf)
            ms = self.dma("sp", bass.AP(cache, idx * 128 * width,
                                        [[width, 128], [1, width]]),
                          dst_ap, self.dS, waits=[m])
            if not hasattr(self, "cache_stores"):
                self.cache_stores = {}
            self.cache_stores[(cache.name, idx)] = ms
            return ms
        m = self.dma("gp", dst_ap,
                     bass.AP(cache, idx * 128 * width, [[width, 128], [1, width]]),
                     self.dW,
                     waits=list(wf) + ([self.cache_stores.get((cache.name, idx))]
                                       if self.ffn_pass == 1 else []))
        return m

    def p_epilogue(self):
        self.join()
        m = self.dma("sp", self.y_out[:, :], self.X[:, :], self.dS)
        self.raw_wait("sp", [m])



_CACHE = {}


def _perm_a():
    p = np.arange(1024)
    return (p % 64) * 16 + p // 64


def _lhsT_flat(wT):
    K, M = wT.shape
    s = wT.reshape(K // 128, 128, M // 128, 128).transpose(2, 1, 0, 3)
    return np.ascontiguousarray(
        s.reshape(M // 128, 128, K).transpose(1, 0, 2).reshape(128, (M // 128) * K))


def _lhsT_strips(wT):
    K, M = wT.shape
    s = wT.reshape(K // 128, 128, M // 128, 128).transpose(2, 1, 0, 3)
    return np.ascontiguousarray(s.reshape((M // 128) * 128, K))


def _rhs_flat(wT):
    K, M = wT.shape
    return np.ascontiguousarray(
        wT.reshape(K // 128, 128, M).transpose(1, 0, 2).reshape(128, (K // 128) * M))


def _sc(v, nch):
    return np.ascontiguousarray(v.reshape(nch, 128).T)


def _build_in_maps(inputs):
    f32 = np.float32
    perm = _perm_a()
    qw = np.asarray(inputs["L0_attn_a_qw"], f32)[:, perm]
    kw = np.asarray(inputs["L0_attn_a_kw"], f32)[:, perm]
    vw = np.asarray(inputs["L0_attn_a_vw"], f32)[:, perm]
    ow = np.asarray(inputs["L0_attn_a_ow"], f32)[perm, :]
    wqkv = np.asarray(inputs["L0_attn_b_w_qkv"], f32)
    bqkv = np.asarray(inputs["L0_attn_b_b_qkv"], f32)
    wout = np.asarray(inputs["L0_attn_b_w_out"], f32)
    bout = np.asarray(inputs["L0_attn_b_b_out"], f32)
    x = np.asarray(inputs["L0_attn_a_x"], f32)

    owa = _lhsT_flat(ow)
    owb = _lhsT_flat(np.ascontiguousarray(wout.T))
    w1 = _lhsT_strips(np.ascontiguousarray(np.asarray(inputs["L0_ffn_w1_w"], f32).T))
    w2 = _lhsT_strips(np.ascontiguousarray(np.asarray(inputs["L0_ffn_w2_w"], f32).T))
    h1 = _lhsT_strips(np.ascontiguousarray(np.asarray(inputs["L0_ffn_h1_w"], f32).T))
    h2 = _lhsT_strips(np.ascontiguousarray(np.asarray(inputs["L0_ffn_h2_w"], f32).T))
    w3 = _lhsT_strips(np.ascontiguousarray(np.asarray(inputs["L0_ffn_w3_w"], f32).T))
    s1 = _sc(np.asarray(inputs["L0_ffn_w1_s"], f32), NHC)
    s2 = _sc(np.asarray(inputs["L0_ffn_w2_s"], f32), NHC)
    sh1 = _sc(np.asarray(inputs["L0_ffn_h1_s"], f32), NHC)
    sh2 = _sc(np.asarray(inputs["L0_ffn_h2_s"], f32), NHC)
    s3 = _sc(np.asarray(inputs["L0_ffn_w3_s"], f32), NCH)

    kl = np.arange(128)[:, None]
    cl = np.arange(1920)[None, :]
    maskA = (kl <= cl - 896).astype(BFD)
    ql = np.arange(128)[None, :]
    maskW = np.ascontiguousarray(np.tile((kl >= ql), (1, 8))).astype(BFD)
    maskA8 = np.ascontiguousarray(np.tile((kl <= ql), (1, 8))).astype(BFD)
    alibi = np.ascontiguousarray(
        np.stack([np.arange(128, dtype=f32) + (-256 + 128 * i) - 64
                  for i in range(3)], axis=1).astype(f32))

    in_maps = []
    for c in range(8):
        b, g = c // 4, c % 4
        xs = np.ascontiguousarray(
            x[b, g * TOK:(g + 1) * TOK, :].T.reshape(NCH, 128, TOK)
            .transpose(1, 0, 2).reshape(128, NCH * TOK))
        sl = slice(g * 256, (g + 1) * 256)
        m = {
            "x0f": xs,
            "qwa": _lhsT_flat(np.ascontiguousarray(qw[:, sl])),
            "kwa": _lhsT_flat(np.ascontiguousarray(kw[:, sl])),
            "vwa": _rhs_flat(np.ascontiguousarray(vw[:, sl])),
            "owa": owa,
            "qwb": _lhsT_flat(np.ascontiguousarray(wqkv[sl, :].T)),
            "kwb": _lhsT_flat(np.ascontiguousarray(
                wqkv[1024 + g * 256: 1024 + (g + 1) * 256, :].T)),
            "vwb": _rhs_flat(np.ascontiguousarray(
                wqkv[2048 + g * 256: 2048 + (g + 1) * 256, :].T)),
            "owb": owb,
            "bqc": _sc(bqkv[sl], 2),
            "bkc": _sc(bqkv[1024 + g * 256: 1024 + (g + 1) * 256], 2),
            "bvc": np.ascontiguousarray(np.broadcast_to(
                bqkv[2048 + g * 256: 2048 + (g + 1) * 256], (128, 256))).astype(f32),
            "boc": _sc(bout, NCH),
            "w1s": w1, "w2s": w2, "h1s": h1, "h2s": h2, "w3s": w3,
            "s1c": s1, "s2c": s2, "sh1c": sh1, "sh2c": sh2, "s3c": s3,
            "maskA": maskA, "maskW": maskW, "maskA8": maskA8, "alibi": alibi,
        }
        in_maps.append(m)
    return in_maps


def kernel(**inputs):
    if "nc" not in _CACHE:
        _CACHE["nc"] = MK().build()
    nc = _CACHE["nc"]
    in_maps = _build_in_maps(inputs)
    res = run_bass_kernel_spmd(nc, in_maps, core_ids=list(range(8)))
    out = np.zeros((2, S, D), dtype=np.float32)
    for c in range(8):
        b, g = c // 4, c % 4
        y = res.results[c]["y0f"]
        out[b, g * TOK:(g + 1) * TOK, :] = (
            y.reshape(128, NCH, TOK).transpose(1, 0, 2).reshape(D, TOK).T)
    return out
